# version 20
# baseline (speedup 1.0000x reference)
import contextlib
import numpy as np
import concourse.bass as bass
import concourse.mybir as mybir
from concourse.bass_utils import run_bass_kernel_spmd

F32 = mybir.dt.float32
F32R = mybir.dt.float32r
BF16 = mybir.dt.bfloat16
I32 = mybir.dt.int32
AF = mybir.ActivationFunctionType
ALU = mybir.AluOpType
AX = mybir.AxisListType

B, L, E, H, HD = 16, 2048, 1024, 16, 64
NCORE = 8
SEQS = 2
T = SEQS * L
CPS = L // 128
NST = 2
CST = 8
NE = 8
EPS_LN = 1e-5
EPS_ATT = 1e-4
PI = float(np.pi)

STAGE = 4
DBG = 0


class Sem:
    __slots__ = ("h", "n")

    def __init__(self, h):
        self.h = h
        self.n = 0

    def inc(self, k=1):
        self.n += k
        return self.n


def build_nc():
    nc = bass.Bass("TRN2", target_bir_lowering=False, debug=False)
    x_in = nc.dram_tensor("x", [T, E], F32, kind="ExternalInput").ap()
    wqkv = nc.dram_tensor("wqkv", [E, 3 * E], F32, kind="ExternalInput").ap()
    bqkv = nc.dram_tensor("bqkv", [1, 3 * E], F32, kind="ExternalInput").ap()
    wout = nc.dram_tensor("wout", [E, E], F32, kind="ExternalInput").ap()
    bout = nc.dram_tensor("bout", [1, E], F32, kind="ExternalInput").ap()
    gate = nc.dram_tensor("gate", [NE, E], F32, kind="ExternalInput").ap()
    expw = nc.dram_tensor("expw", [NE, E, E], F32, kind="ExternalInput").ap()
    out_t = nc.dram_tensor("out", [T, E], F32, kind="ExternalOutput").ap()
    scr_weth = nc.dram_tensor("scr_weth", [NE, 128, 8, E], BF16).ap()
    scr_wetl = nc.dram_tensor("scr_wetl", [NE, 128, 8, E], BF16).ap()
    scr_xtok = nc.dram_tensor("scr_xtok", [32, 128, E], F32).ap()
    scr_xsort = nc.dram_tensor("scr_xsort", [T, 1088], F32).ap()
    scr_y = nc.dram_tensor("scr_y", [T, E], F32).ap()
    if DBG:
        o_pos = nc.dram_tensor("o_pos", [128, 32], mybir.dt.uint32,
                               kind="ExternalOutput").ap()
        o_chosen = nc.dram_tensor("o_chosen", [128, 32], F32,
                                  kind="ExternalOutput").ap()
        o_rows = nc.dram_tensor("o_rows", [1, 256 * 3 + 8 * 5], F32,
                                kind="ExternalOutput").ap()
        o_y = nc.dram_tensor("o_y", [T, E], F32, kind="ExternalOutput").ap()
        o_xs = nc.dram_tensor("o_xs", [T, 1088], F32, kind="ExternalOutput").ap()
        o_kvs = nc.dram_tensor("o_kvs", [128, 2080], F32, kind="ExternalOutput").ap()
    scr_wqh = nc.dram_tensor("scr_wqh", [E, 3 * E], BF16).ap()
    scr_wql = nc.dram_tensor("scr_wql", [E, 3 * E], BF16).ap()
    scr_xnth = nc.dram_tensor("scr_xnth", [32, 128, E], BF16).ap()
    scr_xntl = nc.dram_tensor("scr_xntl", [32, 128, E], BF16).ap()
    scr_acc = nc.dram_tensor("scr_acc", [32, 128, E], F32).ap()

    es = contextlib.ExitStack()
    with es:
        sb = lambda nm, shp, dt=F32: es.enter_context(nc.sbuf_tensor(nm, shp, dt))
        xt2 = sb("xt2", [128, 2048])
        xnc2 = sb("xnc2", [128, 2048])
        xnth = sb("xnth", [128, 8192], BF16)
        xntl = sb("xntl", [128, 8192], BF16)
        acc_st = sb("acc_st", [128, 8192])
        wsth = sb("wsth", [128, 2048], BF16)
        wstl = sb("wstl", [128, 2048], BF16)
        kvt4 = sb("kvt4", [128, 4096])
        poolA = sb("poolA", [128, 12416])
        poolB = sb("poolB", [128, 8192])
        kvstack = sb("kvstack", [128, 2080])
        ident = sb("ident", [128, 128])
        posf = sb("posf", [128, 16])
        posi = sb("posi", [128, 16], I32)
        sintab = sb("sintab", [128, 16])
        costab = sb("costab", [128, 16])
        stats = sb("stats", [128, 12])
        mvt = sb("mvt", [128, 2])
        sq1 = sb("sq1", [128, 1])
        rstd = sb("rstd", [128, 1])
        epsc = sb("epsc", [128, 1])
        pih = sb("pih", [128, 1])
        ones1 = sb("ones1", [1, 128])
        bqkv_col = sb("bqkv_col", [128, 24])
        bout_row = sb("bout_row", [1, E])
        gt_in = sb("gt_in", [8, E])
        gwt = sb("gwt", [128, 64])
        gwth = sb("gwth", [128, 64], BF16)
        gwtl = sb("gwtl", [128, 64], BF16)
        masks_st = sb("masks_st", [128, 256], BF16)
        chosen_all = sb("chosen_all", [128, 32])
        pos_u32 = sb("pos_u32", [128, 32], mybir.dt.uint32)
        iota8f = sb("iota8f", [128, 8])
        iotacol = sb("iotacol", [128, 1])
        iotac4 = sb("iotac4", [128, 4])
        ustrict = sb("ustrict", [128, 128], BF16)
        ustri = sb("ustri", [128, 128], I32)
        onescol = sb("onescol", [128, 1], BF16)
        cnt_row = sb("cnt_row", [1, 256])
        coffx = sb("coffx", [1, 256])
        combo_row = cnt_row
        cumtot = sb("cumtot", [1, 8])
        basex = sb("basex", [1, 8])
        e1row = sb("e1row", [1, 8])
        tbrow = sb("tbrow", [1, 8])
        flagrow = sb("flagrow", [1, 8])
        tmprow = sb("tmprow", [1, 8])
        tmprow2 = sb("tmprow2", [1, 8])
        zrow = sb("zrow", [1, 32])
        e1col = sb("e1col", [128, 8])
        tbcol = sb("tbcol", [128, 8])
        flagcol = sb("flagcol", [128, 8])
        widx = sb("widx", [128, 16], mybir.dt.uint32)
        io2 = sb("io2", [128, 1])
        io2i = sb("io2i", [128, 1], I32)
        xidx = sb("xidx", [128, 28], mybir.dt.uint32)
        xidxf = sb("xidxf", [128, 28])
        tmpcol8 = sb("tmpcol8", [128, 8])
        mx1 = sb("mx1", [128, 1])
        m08 = sb("m08", [128, 8])
        cum8 = sb("cum8", [128, 8])
        meq8 = sb("meq8", [128, 8])
        zero8 = sb("zero8", [128, 8])
        rec16 = sb("rec16", [128, 16])
        rec16b = sb("rec16b", [128, 16])

        pb = [es.enter_context(nc.psum_tensor(f"pb{i}", [128, 512], F32))
              for i in range(8)]

        mksem = lambda nm: Sem(es.enter_context(nc.semaphore(nm)))
        dx = mksem("dx")
        dwst = mksem("dwst")
        dwe = mksem("dwe")
        dmisc = mksem("dmisc")
        dscw = mksem("dscw")
        dxw = mksem("dxw")
        dxr = mksem("dxr")
        daw = mksem("daw")
        dar = mksem("dar")
        dout = mksem("dout")
        dxtw = mksem("dxtw")
        dsc = mksem("dsc")
        dyw = mksem("dyw")
        dfx = mksem("dfx")
        dfw = mksem("dfw")
        dyr = mksem("dyr")
        tte = mksem("tte")
        tve = mksem("tve")
        tse = mksem("tse")
        tgp = mksem("tgp")

        TEq, VEq, SEq, GPq, SYq = [], [], [], [], []

        def w(q, s, v=None):
            vv = s.n if v is None else v
            q.append(lambda e, s=s, vv=vv: e.wait_ge(s.h, vv))

        def emit(q, fn, s=None, k=1):
            if s is None:
                q.append(fn)
            else:
                s.inc(k)
                q.append(lambda e, fn=fn, s=s, k=k: fn(e).then_inc(s.h, k))

        class Rot:
            def __init__(self):
                self.i = 0
                self.free = [None, None, None]

            def acq(self):
                b = self.i % 3
                self.i += 1
                fr = self.free[b]
                if fr is not None:
                    w(TEq, fr[0], fr[1])
                return b

            def rel(self, b, sem):
                self.free[b] = (sem, sem.n)

        rot = Rot()
        class TRot:
            def __init__(self):
                self.i = 0
                self.free = [None, None]

            def acq(self):
                b = self.i % 2
                self.i += 1
                fr = self.free[b]
                if fr is not None:
                    w(TEq, fr[0], fr[1])
                return 3 + b

            def rel(self, b, sem):
                self.free[b - 3] = (sem, sem.n)

        trot = TRot()

        def ve_join():
            w(VEq, tve)

        def se_join():
            w(SEq, tse)

        def bar():
            w(TEq, tve)
            w(TEq, tse)
            w(VEq, tte)
            w(VEq, tse)
            w(SEq, tte)
            w(SEq, tve)

        emit(SYq, lambda e: e.dma_start(out=gt_in[:], in_=gate[:]), dmisc, 16)
        emit(SYq, lambda e: e.dma_start(
            out=bqkv_col[:], in_=bqkv.rearrange("o (c p) -> (o p) c", p=128),
            allow_slow_non_contiguous=True),
            dmisc, 16)
        emit(SYq, lambda e: e.dma_start(out=bout_row[:], in_=bout[:]), dmisc, 16)

        emit(GPq, lambda e: e.iota(ident[:].bitcast(I32), pattern=[[-1, 128]],
                                   base=0, channel_multiplier=1))
        emit(GPq, lambda e: e.iota(posi[:], pattern=[[128, 16]],
                                   base=0, channel_multiplier=1))
        emit(GPq, lambda e: e.iota(iota8f[:].bitcast(I32), pattern=[[1, 8]],
                                   base=0, channel_multiplier=0))
        emit(GPq, lambda e: e.iota(iotacol[:].bitcast(I32), pattern=[[1, 1]],
                                   base=0, channel_multiplier=1))
        emit(GPq, lambda e: e.iota(io2i[:], pattern=[[1, 1]],
                                   base=0, channel_multiplier=2))
        emit(GPq, lambda e: e.iota(iotac4[:].bitcast(I32), pattern=[[128, 4]],
                                   base=0, channel_multiplier=1))
        emit(GPq, lambda e: e.iota(ustri[:], pattern=[[-1, 128]],
                                   base=0, channel_multiplier=1))
        emit(GPq, lambda e: e.memset(onescol[:], 1.0))
        emit(GPq, lambda e: e.memset(zrow[:], 0.0))
        emit(GPq, lambda e: e.memset(ones1[:], 1.0))
        emit(GPq, lambda e: e.memset(epsc[:], EPS_LN))
        emit(GPq, lambda e: e.memset(pih[:], PI / 2))
        emit(GPq, lambda e: e.memset(zero8[:], 0.0), tgp)

        w(VEq, tgp)
        emit(VEq, lambda e: e.tensor_single_scalar(
            ident[:], ident[:].bitcast(I32), 0, ALU.is_equal), tve)
        emit(VEq, lambda e: e.tensor_single_scalar(
            ustrict[:], ustri[:], 0, ALU.is_lt), tve)
        emit(VEq, lambda e: e.tensor_copy(iota8f[:], iota8f[:].bitcast(I32)), tve)
        emit(VEq, lambda e: e.tensor_copy(iotacol[:], iotacol[:].bitcast(I32)), tve)
        emit(VEq, lambda e: e.tensor_copy(io2[:], io2i[:]), tve)
        emit(VEq, lambda e: e.tensor_copy(iotac4[:], iotac4[:].bitcast(I32)), tve)
        ve_join()
        emit(VEq, lambda e: e.tensor_copy(posf[:], posi[:]), tve)

        w(SEq, tve)
        emit(SEq, lambda e: e.activation(sintab[:], posf[:], AF.Sin,
                                         scale=PI / (2.0 * L)), tse)
        emit(SEq, lambda e: e.activation(costab[:], posf[:], AF.Sin,
                                         bias=pih[:], scale=-PI / (2.0 * L)), tse)

        w(TEq, tve)
        w(TEq, dmisc, 16)
        for dc in range(8):
            emit(TEq, lambda e, dc=dc: e.transpose(
                pb[3][:, dc * 8:dc * 8 + 8], gt_in[0:8, dc * 128:dc * 128 + 128],
                ident[0:8, 0:8]), tte if dc == 7 else None)
        w(VEq, tte)
        emit(VEq, lambda e: e.tensor_copy(gwt[:], pb[3][:, 0:64]), tve)
        w(SEq, tve)
        emit(SEq, lambda e: e.copy(gwth[:], gwt[:]), tse)
        w(VEq, tse)
        emit(VEq, lambda e: e.tensor_sub(gwtl[:], gwt[:], gwth[:]), tve)
        gwt_ready = tve.n
        gwt_ready_se = tse.n

        pAb = poolA[:].bitcast(BF16)
        wsplit = poolA[:, 8192:11264]
        wsp_free = [None, None]
        for wc in range(8):
            par = wc % 2
            emit(SYq, lambda e, wc=wc: e.dma_start(
                out=wsplit[:], in_=wqkv[wc * 128:wc * 128 + 128, :]), dmisc, 16)
            wld = dmisc.n
            hi = pAb[:, par * 6144:par * 6144 + 3072]
            lo = pAb[:, par * 6144 + 3072:par * 6144 + 6144]
            w(SEq, dmisc, wld)
            if wsp_free[par] is not None:
                fs, fv = wsp_free[par]
                w(SEq, fs, fv)
            emit(SEq, lambda e, hi=hi: e.copy(hi, wsplit[:]), tse)
            w(VEq, tse)
            emit(VEq, lambda e, hi=hi, lo=lo: e.tensor_sub(
                lo, wsplit[:], hi), tve)
            w(SYq, tve)
            w(GPq, tve)
            emit(GPq, lambda e, wc=wc, hi=hi: e.dma_start(
                out=scr_wqh[wc * 128:wc * 128 + 128, :], in_=hi), dscw, 16)
            emit(GPq, lambda e, wc=wc, lo=lo: e.dma_start(
                out=scr_wql[wc * 128:wc * 128 + 128, :], in_=lo), dscw, 16)
            wsp_free[par] = (dscw, dscw.n)

        if STAGE >= 2:
            wets_h = [pAb[:, 12288:13312], pAb[:, 14336:15360]]
            wets_l = [pAb[:, 13312:14336], pAb[:, 15360:16384]]
            wets_free = [None, None]
            for ei in range(NE):
                if ei > 0:
                    w(SYq, tte)
                emit(SYq, lambda e, ei=ei: e.dma_start(
                    out=poolB[:],
                    in_=expw[ei].rearrange("(c p) h -> p c h", p=128)),
                    dwe, 16)
                w(TEq, dwe)
                for hc in range(8):
                    par = (ei * 8 + hc) % 2
                    if wets_free[par] is not None:
                        fs, fv = wets_free[par]
                    tb3 = trot.acq()
                    for fc in range(4):
                        emit(TEq, lambda e, fc=fc, hc=hc, tb3=tb3: e.transpose(
                            pb[tb3][:, fc * 128:fc * 128 + 128],
                            poolB[:, fc * 1024 + hc * 128:fc * 1024 + hc * 128 + 128],
                            ident[:]), None)
                    tb4 = trot.acq()
                    for fc in range(4):
                        emit(TEq, lambda e, fc=fc, hc=hc, tb4=tb4: e.transpose(
                            pb[tb4][:, fc * 128:fc * 128 + 128],
                            poolB[:, (fc + 4) * 1024 + hc * 128:(fc + 4) * 1024 + hc * 128 + 128],
                            ident[:]), tte)
                    w(SEq, tte)
                    if wets_free[par] is not None:
                        fs, fv = wets_free[par]
                        w(SEq, fs, fv)
                        w(VEq, fs, fv)
                    emit(SEq, lambda e, par=par, tb3=tb3: e.copy(
                        wets_h[par][:, 0:512], pb[tb3][:]), tse)
                    emit(SEq, lambda e, par=par, tb4=tb4: e.copy(
                        wets_h[par][:, 512:1024], pb[tb4][:]), tse)
                    w(VEq, tse)
                    emit(VEq, lambda e, par=par, tb3=tb3: e.tensor_sub(
                        wets_l[par][:, 0:512], pb[tb3][:],
                        wets_h[par][:, 0:512]), tve)
                    emit(VEq, lambda e, par=par, tb4=tb4: e.tensor_sub(
                        wets_l[par][:, 512:1024], pb[tb4][:],
                        wets_h[par][:, 512:1024]), tve)
                    trot.rel(tb3, tve)
                    trot.rel(tb4, tve)
                    w(GPq, tve)
                    emit(GPq, lambda e, ei=ei, hc=hc, par=par: e.dma_start(
                        out=scr_weth[ei, :, hc, :], in_=wets_h[par][:]), dscw, 16)
                    emit(GPq, lambda e, ei=ei, hc=hc, par=par: e.dma_start(
                        out=scr_wetl[ei, :, hc, :], in_=wets_l[par][:]), dscw, 16)
                    wets_free[par] = (dscw, dscw.n)

        qrelT = poolA[:, 0:4096]
        qrel_tok = poolA[:, 4096:5120]
        qsc = poolA[:, 5120:7168]
        qstack = poolA[:, 7168:9216]
        attn = poolA[:, 9216:10240]
        attnT = poolA[:, 10240:11264]
        krel = poolA[:, 0:4096]
        vcs = poolA[:, 4096:12416]

        kv_first = {}
        wst_ctr = [0]
        xnt_free = (tte, 0)
        acc_free = (tve, 0)
        poolB_free = (tte, tte.n)
        poolA_free = (tte, 0)
        xt_free = [None, None]
        xnc_free = [None, None]
        xnc_free2 = [None, None]
        wst_free = [None, None]
        pb_exp_free = [None, None]
        xnt_ready = {}
        last_accw = {}

        def xnt_ap(ten, dc, tokgroups, base_ch):
            v = ten.rearrange("p (c q) -> p c q", q=1024)
            return v[:, base_ch:base_ch + tokgroups, dc * 128:dc * 128 + 128]

        def plan_chunk_A(c, i, u_par, seq, sti):
            par = i % 2
            if xt_free[par] is not None:
                fs, fv = xt_free[par]
                w(SYq, fs, fv)
            emit(SYq, lambda e, c=c, par=par: e.dma_start(
                out=xt2[:, par * 1024:par * 1024 + 1024],
                in_=x_in[c * 128:(c + 1) * 128, :]), dx, 16)
            xload = dx.n
            xs = xt2[:, par * 1024:par * 1024 + 1024]
            xn = xnc2[:, par * 1024:par * 1024 + 1024]
            w(VEq, dx, xload)
            emit(VEq, lambda e, xs=xs: e.bn_stats(stats[:, 0:6], xs[:, 0:512]), tve)
            emit(VEq, lambda e, xs=xs: e.bn_stats(stats[:, 6:12], xs[:, 512:1024]), tve)
            ve_join()
            emit(VEq, lambda e: e.bn_aggr(mvt[:], stats[:]), tve)
            w(SEq, tve)
            emit(SEq, lambda e: e.activation(sq1[:], mvt[:, 1:2], AF.Sqrt,
                                             bias=epsc[:]), tse)
            w(VEq, tse)
            emit(VEq, lambda e: e.reciprocal(rstd[:], sq1[:]), tve)
            ve_join()
            if xnc_free[par] is not None:
                fs, fv = xnc_free[par]
                w(VEq, fs, fv)
            if xnc_free2[par] is not None:
                fs, fv = xnc_free2[par]
                w(VEq, fs, fv)
            emit(VEq, lambda e, xs=xs, xn=xn: e.tensor_scalar(
                xn, xs, mvt[:, 0:1], rstd[:], ALU.subtract, ALU.mult), tve)
            ve_join()
            emit(VEq, lambda e, xs=xs, xn=xn, i=i: e.tensor_add(
                acc_st[:, i * 1024:i * 1024 + 1024], xs, xn), tve)
            xt_free[par] = (tve, tve.n)
            w(TEq, tve)
            b3 = trot.acq()
            for dc in range(4):
                emit(TEq, lambda e, dc=dc, xn=xn, b3=b3: e.transpose(
                    pb[b3][:, dc * 128:dc * 128 + 128],
                    xn[:, dc * 128:dc * 128 + 128], ident[:]), None)
            b4 = trot.acq()
            for dc in range(4):
                emit(TEq, lambda e, dc=dc, xn=xn, b4=b4: e.transpose(
                    pb[b4][:, dc * 128:dc * 128 + 128],
                    xn[:, (dc + 4) * 128:(dc + 4) * 128 + 128], ident[:]), tte)
            w(GPq, tve)
            emit(GPq, lambda e, c=c, xn=xn: e.dma_start(
                out=scr_xtok[c], in_=xn), dxtw, 16)
            xnc_free[par] = (tte, tte.n)
            xnc_free2[par] = (dxtw, dxtw.n)
            w(SEq, tte)
            emit(SEq, lambda e, i=i, b3=b3: e.copy(
                xnth[:, i * 1024:i * 1024 + 512], pb[b3][:]), tse)
            emit(SEq, lambda e, i=i, b4=b4: e.copy(
                xnth[:, i * 1024 + 512:i * 1024 + 1024], pb[b4][:]), tse)
            w(VEq, tse)
            emit(VEq, lambda e, i=i, b3=b3: e.tensor_sub(
                xntl[:, i * 1024:i * 1024 + 512], pb[b3][:],
                xnth[:, i * 1024:i * 1024 + 512]), tve)
            emit(VEq, lambda e, i=i, b4=b4: e.tensor_sub(
                xntl[:, i * 1024 + 512:i * 1024 + 1024], pb[b4][:],
                xnth[:, i * 1024 + 512:i * 1024 + 1024]), tve)
            trot.rel(b3, tve)
            trot.rel(b4, tve)
            xnt_ready[c] = (tve, tve.n)
            w(GPq, tve)
            emit(GPq, lambda e, c=c, i=i: e.dma_start(
                out=scr_xnth[c], in_=xnth[:, i * 1024:i * 1024 + 1024]), dxw, 16)
            emit(GPq, lambda e, c=c, i=i: e.dma_start(
                out=scr_xntl[c], in_=xntl[:, i * 1024:i * 1024 + 1024]), dxw, 16)

        def plan_kv_part(g, isv, c0, seq):
            evac_groups = []
            for ocl in range(4):
                oc = g * 4 + ocl
                ocbase = (2048 if isv else 1024) + oc * 128
                parW = wst_ctr[0] % 2
                wst_ctr[0] += 1
                if wst_free[parW] is not None:
                    fs, fv = wst_free[parW]
                    w(SYq, fs, fv)
                emit(SYq, lambda e, ocbase=ocbase, parW=parW: e.dma_start(
                    out=wsth[:, parW * 1024:parW * 1024 + 1024],
                    in_=scr_wqh.rearrange("(dc p) q -> p dc q", p=128)[
                        :, :, ocbase:ocbase + 128]), dwst, 16)
                emit(SYq, lambda e, ocbase=ocbase, parW=parW: e.dma_start(
                    out=wstl[:, parW * 1024:parW * 1024 + 1024],
                    in_=scr_wql.rearrange("(dc p) q -> p dc q", p=128)[
                        :, :, ocbase:ocbase + 128]), dwst, 16)
                wload = dwst.n
                for tg in range(2):
                    mb = rot.acq()
                    w(TEq, dwst, wload)
                    for dc in range(8):
                        for pi, (wt, xt_) in enumerate(
                                ((wsth, xnth), (wsth, xntl), (wstl, xnth))):
                            emit(TEq, lambda e, dc=dc, parW=parW, tg=tg, mb=mb,
                                 wt=wt, xt_=xt_, pi=pi: e.matmul(
                                     pb[mb][:],
                                     wt[:, parW * 1024 + dc * 128:
                                        parW * 1024 + dc * 128 + 128],
                                     xnt_ap(xt_, dc, 4, tg * 4),
                                     start=(dc == 0 and pi == 0),
                                     stop=(dc == 7 and pi == 2)),
                                 tte if (dc == 7 and pi == 2) else None)
                    bcol = (16 if isv else 8) + oc
                    w(SEq, tte)
                    emit(SEq, lambda e, mb=mb, ocl=ocl, tg=tg, bcol=bcol, isv=isv:
                         e.activation(
                             kvt4[:, ocl * 1024 + tg * 512:ocl * 1024 + tg * 512 + 512],
                             pb[mb][:], AF.Identity if isv else AF.Relu,
                             bias=bqkv_col[:, bcol:bcol + 1]), tse)
                    rot.rel(mb, tse)
                wst_free[parW] = (tte, tte.n)
            kvt_ready = tse.n
            w(TEq, tse, kvt_ready)
            for i in range(CST):
                tb = trot.acq()
                for ocl in range(4):
                    emit(TEq, lambda e, ocl=ocl, i=i, tb=tb: e.transpose(
                        pb[tb][:, ocl * 128:ocl * 128 + 128],
                        kvt4[:, ocl * 1024 + i * 128:ocl * 1024 + i * 128 + 128],
                        ident[:]), tte if ocl == 3 else None)
                posidx = (c0 % CPS) + i
                if not isv:
                    w(VEq, tte)
                    emit(VEq, lambda e, i=i, tb=tb: e.tensor_copy(
                        krel[:, i * 512:i * 512 + 512], pb[tb][:]), tve)
                    trot.rel(tb, tve)
                else:
                    w(SEq, tte)
                    for trig in range(2):
                        tab = costab if trig == 0 else sintab
                        base = trig * 4160 + i * 520
                        emit(SEq, lambda e, tb=tb, base=base, tab=tab, posidx=posidx:
                             e.activation(
                                 vcs[:, base:base + 520].rearrange(
                                     "p (h j) -> p h j", j=65)[:, :, 0:64],
                                 pb[tb][:].rearrange(
                                     "p (h j) -> p h j", j=64),
                                 AF.Identity, scale=tab[:, posidx:posidx + 1]),
                             tse)
                    trot.rel(tb, tse)
                    w(VEq, tse)
                    for trig in range(2):
                        tab = costab if trig == 0 else sintab
                        base = trig * 4160 + i * 520
                        emit(VEq, lambda e, base=base, tab=tab, posidx=posidx:
                             e.tensor_copy(
                                 vcs[:, base:base + 520].rearrange(
                                     "p (h j) -> p h j", j=65)[:, :, 64:65],
                                 tab[:, posidx:posidx + 1].to_broadcast([128, 8, 1])),
                             tve)

        def plan_kv_einsum(g, seq, is_last_supertile):
            w(TEq, tve)
            w(TEq, tse)
            for i in range(CST):
                for hl in range(8):
                    h = g * 8 + hl
                    bk = 5 + (h // 7)
                    co = (h % 7) * 65
                    for trig in range(2):
                        first = kv_first[(bk - 5, trig)]
                        kv_first[(bk - 5, trig)] = False
                        emit(TEq, lambda e, i=i, hl=hl, bk=bk, co=co, trig=trig,
                             first=first: e.matmul(
                                 pb[bk][0:64, co:co + 65] if trig == 0
                                 else pb[bk][64:128, co:co + 65],
                                 krel[:, i * 512 + hl * 64:i * 512 + hl * 64 + 64],
                                 vcs[:, trig * 4160 + i * 520 + hl * 65:
                                     trig * 4160 + i * 520 + hl * 65 + 65],
                                 start=first, stop=False,
                                 tile_position=(0, 64 * trig),
                                 skip_group_check=True),
                             tte if (hl == 7 and trig == 1) else None)

        def plan_gate_masks(c0):
            for i in range(CST):
                c = c0 + i
                mb = rot.acq()
                rs, rv = xnt_ready[c0 + i]
                w(TEq, rs, rv)
                w(TEq, tve, gwt_ready)
                w(TEq, tse, gwt_ready_se)
                for dc in range(8):
                    for pi, (xt_, gt_) in enumerate(
                            ((xnth, gwth), (xnth, gwtl), (xntl, gwth))):
                        emit(TEq, lambda e, i=i, dc=dc, mb=mb, xt_=xt_, gt_=gt_,
                             pi=pi: e.matmul(
                                 pb[mb][:, 0:8],
                                 xt_[:, i * 1024 + dc * 128:i * 1024 + dc * 128 + 128],
                                 gt_[:, dc * 8:dc * 8 + 8],
                                 start=(dc == 0 and pi == 0),
                                 stop=(dc == 7 and pi == 2)),
                             tte if (dc == 7 and pi == 2) else None)
                w(VEq, tte)
                emit(VEq, lambda e, mb=mb: e.reduce_max(mx1[:], pb[mb][:, 0:8],
                                                        axis=AX.X), tve)
                ve_join()
                emit(VEq, lambda e, mb=mb: e.tensor_scalar(
                    m08[:], pb[mb][:, 0:8], mx1[:, 0:1], None, ALU.is_equal), tve)
                rot.rel(mb, tve)
                ve_join()
                emit(VEq, lambda e: e.tensor_tensor_scan(
                    cum8[:], m08[:], zero8[:], 0.0, ALU.add, ALU.add), tve)
                ve_join()
                emit(VEq, lambda e: e.tensor_single_scalar(
                    meq8[:], cum8[:], 1.0, ALU.is_equal), tve)
                ve_join()
                emit(VEq, lambda e, c=c: e.tensor_mul(
                    masks_st[:, c * 8:c * 8 + 8], m08[:], meq8[:]), tve)
                ve_join()
                emit(VEq, lambda e, c=c: e.tensor_mul(
                    tmpcol8[:], masks_st[:, c * 8:c * 8 + 8], iota8f[:]), tve)
                ve_join()
                emit(VEq, lambda e, c=c: e.reduce_sum(
                    chosen_all[:, c:c + 1], tmpcol8[:], axis=AX.X), tve)
                masks_rdy = tve.n
                mbx = rot.acq()
                w(TEq, tve, masks_rdy)
                emit(TEq, lambda e, c=c, mbx=mbx: e.matmul(
                    pb[mbx][0:1, 0:8], onescol[:],
                    masks_st[:, c * 8:c * 8 + 8],
                    start=True, stop=True), tte)
                w(VEq, tte)
                emit(VEq, lambda e, c=c, mbx=mbx: e.tensor_copy(
                    cnt_row[0:1, c * 8:c * 8 + 8], pb[mbx][0:1, 0:8]), tve)
                rot.rel(mbx, tve)


        from concourse.bass import IndirectOffsetOnAxis

        def plan_bookkeeping():
            ve_join()
            cr = cnt_row.rearrange("o (c ee) -> o ee c", ee=8)
            cf = coffx.rearrange("o (c ee) -> o ee c", ee=8)
            for ee in range(8):
                emit(VEq, lambda e, ee=ee: e.tensor_tensor_scan(
                    cf[:, ee, :], cr[:, ee, :], zrow[:], 0.0,
                    ALU.add, ALU.add), tve)
            ve_join()
            emit(VEq, lambda e: e.tensor_copy(tmprow[:], coffx[0:1, 248:256]), tve)
            ve_join()
            emit(VEq, lambda e: e.tensor_sub(coffx[:], coffx[:], cnt_row[:]), tve)
            emit(VEq, lambda e: e.tensor_tensor_scan(
                cumtot[:], tmprow[:], zrow[0:1, 0:8], 0.0, ALU.add, ALU.add), tve)
            ve_join()
            emit(VEq, lambda e: e.tensor_sub(basex[:], cumtot[:], tmprow[:]), tve)
            ve_join()
            emit(VEq, lambda e: e.tensor_add(
                combo_row.rearrange("o (c ee) -> o c ee", ee=8),
                coffx.rearrange("o (c ee) -> o c ee", ee=8),
                basex[:].unsqueeze(1).to_broadcast([1, 32, 8])), tve)
            for j in range(8):
                ve_join()
                emit(VEq, lambda e, j=j: e.tensor_single_scalar(
                    tmprow2[:], cumtot[:], float(j * 512), ALU.is_le), tve)
                ve_join()
                emit(VEq, lambda e, j=j: e.reduce_sum(
                    e1row[0:1, j:j + 1], tmprow2[:], axis=AX.X), tve)
            ve_join()
            emit(VEq, lambda e: e.tensor_single_scalar(
                tbrow[:], basex[:], 512.0, ALU.is_ge), tve)
            for jj in range(2, 8):
                ve_join()
                emit(VEq, lambda e, jj=jj: e.tensor_single_scalar(
                    tmprow2[:], basex[:], float(jj * 512), ALU.is_ge), tve)
                ve_join()
                emit(VEq, lambda e: e.tensor_add(tbrow[:], tbrow[:], tmprow2[:]),
                     tve)
            ve_join()
            emit(VEq, lambda e: e.tensor_scalar(
                tmprow2[:], tbrow[:], 512.0, None, ALU.mult), tve)
            ve_join()
            emit(VEq, lambda e: e.tensor_sub(tmprow2[:], basex[:], tmprow2[:]), tve)
            ve_join()
            emit(VEq, lambda e: e.tensor_single_scalar(
                flagrow[:], tmprow2[:], 0.5, ALU.is_ge), tve)
            rows_ready = tve.n
            for (row, col) in ((e1row, e1col), (tbrow, tbcol), (flagrow, flagcol)):
                mb = rot.acq()
                w(TEq, tve, rows_ready)
                emit(TEq, lambda e, row=row, mb=mb: e.matmul(
                    pb[mb][:, 0:8], ones1[:], row[:],
                    start=True, stop=True), tte)
                w(VEq, tte)
                emit(VEq, lambda e, col=col, mb=mb: e.tensor_copy(
                    col[:], pb[mb][:, 0:8]), tve)
                rot.rel(mb, tve)
            stg_free = [None, None]
            w(SYq, dxtw)
            w(SYq, tte)
            w(SYq, tve)
            for c in range(32):
                mb = rot.acq()
                emit(TEq, lambda e, c=c, mb=mb: e.matmul(
                    pb[mb][:, 0:8], ustrict[:],
                    masks_st[:, c * 8:c * 8 + 8],
                    start=True, stop=True), tte)
                emit(TEq, lambda e, c=c, mb=mb: e.matmul(
                    pb[mb][:, 8:16], ones1[:], combo_row[0:1, c * 8:c * 8 + 8],
                    start=False, stop=True, skip_group_check=True), tte)
                w(VEq, tte)
                ve_join()
                emit(VEq, lambda e, c=c, mb=mb: e.tensor_copy(
                    tmpcol8[:], pb[mb][:, 0:8]), tve)
                ve_join()
                emit(VEq, lambda e, c=c, mb=mb: e.tensor_add(
                    tmpcol8[:], tmpcol8[:], pb[mb][:, 8:16]), tve)
                rot.rel(mb, tve)
                ve_join()
                emit(VEq, lambda e, c=c: e.tensor_mul(
                    tmpcol8[:], tmpcol8[:], masks_st[:, c * 8:c * 8 + 8]), tve)
                ve_join()
                emit(VEq, lambda e, c=c: e.reduce_sum(
                    sq1[:], tmpcol8[:], axis=AX.X), tve)
                ve_join()
                emit(VEq, lambda e, c=c: e.tensor_copy(
                    pos_u32[:, c:c + 1], sq1[:]), tve)
                par = c % 2
                stage = poolA[:, par * 1088:par * 1088 + 1088]
                if stg_free[par] is not None:
                    fs, fv = stg_free[par]
                    w(SYq, fs, fv)
                emit(SYq, lambda e, c=c, stage=stage: e.dma_start(
                    out=stage[:, 0:1024], in_=scr_xtok[c]), dfx, 16)
                ld = dfx.n
                w(VEq, dfx, ld)
                ve_join()
                emit(VEq, lambda e, c=c, stage=stage: e.tensor_copy(
                    stage[:, 1024:1025], chosen_all[:, c:c + 1]), tve)
                w(GPq, tve)
                if c >= 1:
                    w(GPq, dsc, 16 * c - 16)
                emit(GPq, lambda e, c=c, stage=stage: e.indirect_dma_start(
                    out=scr_xsort,
                    out_offset=IndirectOffsetOnAxis(pos_u32[:, c:c + 1], 0),
                    in_=stage[:], in_offset=None), dsc, 16)
                stg_free[par] = (dsc, dsc.n)
            for half in range(2):
                ve_join()
                emit(VEq, lambda e, half=half: e.tensor_scalar(
                    tmpcol8[:], e1col[:], 256.0, float(half),
                    ALU.mult, ALU.add), tve)
                ve_join()
                emit(VEq, lambda e, half=half: e.tensor_tensor(
                    widx[:, half * 8:half * 8 + 8], tmpcol8[:],
                    io2[:].to_broadcast([128, 8]), ALU.add), tve)
            for ee in range(1, 8):
                ve_join()
                emit(VEq, lambda e, ee=ee: e.tensor_scalar(
                    xidxf[:, (ee - 1) * 4:(ee - 1) * 4 + 4],
                    tbcol[:, ee:ee + 1].to_broadcast([128, 4]),
                    512.0, None, ALU.mult), tve)
                ve_join()
                emit(VEq, lambda e, ee=ee: e.tensor_add(
                    xidxf[:, (ee - 1) * 4:(ee - 1) * 4 + 4],
                    xidxf[:, (ee - 1) * 4:(ee - 1) * 4 + 4], iotac4[:]), tve)
            ve_join()
            emit(VEq, lambda e: e.tensor_copy(xidx[:], xidxf[:]), tve)

        def plan_ffn():
            pbB = poolB[:].bitcast(BF16)
            ffx = [poolA[:, 0:1088], poolA[:, 1088:2176]]
            fxnth4 = poolA[:, 2176:4224].bitcast(BF16)
            fxntl4 = poolA[:, 4224:6272].bitcast(BF16)
            ffy4s = [poolA[:, 6272:10368], kvt4[:, 0:4096]]
            mcol4 = poolA[:, 10368:10372]
            wet_rows_h = scr_weth.rearrange(
                "ee p (hh hl) f -> (ee p hh) hl f", hh=2)
            wet_rows_l = scr_wetl.rearrange(
                "ee p (hh hl) f -> (ee p hh) hl f", hh=2)
            pbq_free = [None, None]
            ldctr = [0]
            ffx_free = [None, None]
            ffx_free_te = [None, None]
            fixed_y_done = None

            def load_weights(ent, half):
                buf = ldctr[0] % 2
                ldctr[0] += 1
                q = GPq if ent < 8 else SYq
                if pbq_free[buf] is not None:
                    fs, fv = pbq_free[buf]
                    w(q, fs, fv)
                hi_sl = pbB[:, buf * 8192:buf * 8192 + 4096]
                lo_sl = pbB[:, buf * 8192 + 4096:buf * 8192 + 8192]
                if ent < 8:
                    w(GPq, tve)
                    emit(GPq, lambda e, ent=ent, half=half, hi_sl=hi_sl:
                         e.indirect_dma_start(
                             out=hi_sl,
                             in_=wet_rows_h,
                             out_offset=None,
                             in_offset=IndirectOffsetOnAxis(
                                 widx[:, half * 8 + ent:half * 8 + ent + 1],
                                 0)), dfw, 16)
                    emit(GPq, lambda e, ent=ent, half=half, lo_sl=lo_sl:
                         e.indirect_dma_start(
                             out=lo_sl,
                             in_=wet_rows_l,
                             out_offset=None,
                             in_offset=IndirectOffsetOnAxis(
                                 widx[:, half * 8 + ent:half * 8 + ent + 1],
                                 0)), dfw, 16)
                else:
                    ee = ent - 7
                    emit(SYq, lambda e, ee=ee, half=half, hi_sl=hi_sl:
                         e.dma_start(out=hi_sl,
                                     in_=scr_weth[ee, :, half * 4:half * 4 + 4, :]),
                         dfw, 16)
                    emit(SYq, lambda e, ee=ee, half=half, lo_sl=lo_sl:
                         e.dma_start(out=lo_sl,
                                     in_=scr_wetl[ee, :, half * 4:half * 4 + 4, :]),
                         dfw, 16)
                return buf, dfw.n

            def ffn_bar():
                bar()
                w(GPq, tve)
                w(GPq, tse)
                w(GPq, tte)
                w(SYq, tve)
                w(SYq, tse)
                w(SYq, tte)

            dyw_hist = [0, 0]
            ffy_free = [None, None]
            for ent in range(15):
                fixed = ent < 8
                ee = ent - 7 if not fixed else None
                ffy4 = ffy4s[ent % 2]
                w(GPq, dyw, dyw_hist[1])
                wl0 = load_weights(ent, 0)
                wl1 = load_weights(ent, 1)
                for ic in range(4):
                    par = ic % 2
                    if ffx_free[par] is not None:
                        fs, fv = ffx_free[par]
                        w(SYq if fixed else GPq, fs, fv)
                    if ffx_free_te[par] is not None:
                        fs, fv = ffx_free_te[par]
                        w(SYq if fixed else GPq, fs, fv)
                    if fixed:
                        w(SYq, dsc)
                        emit(SYq, lambda e, ent=ent, ic=ic, par=par: e.dma_start(
                            out=ffx[par][:],
                            in_=scr_xsort[ent * 512 + ic * 128:
                                          ent * 512 + ic * 128 + 128, :]), dfx, 16)
                    else:
                        w(GPq, dsc)
                        emit(GPq, lambda e, ee=ee, ic=ic, par=par:
                             e.indirect_dma_start(
                                 out=ffx[par][:], out_offset=None,
                                 in_=scr_xsort,
                                 in_offset=IndirectOffsetOnAxis(
                                     xidx[:, (ee - 1) * 4 + ic:
                                          (ee - 1) * 4 + ic + 1], 0)), dfx, 16)
                    xld = dfx.n
                    w(TEq, dfx, xld)
                    b3 = trot.acq()
                    for dc in range(4):
                        emit(TEq, lambda e, dc=dc, par=par, b3=b3: e.transpose(
                            pb[b3][:, dc * 128:dc * 128 + 128],
                            ffx[par][:, dc * 128:dc * 128 + 128], ident[:]), None)
                    b4 = trot.acq()
                    for dc in range(4):
                        emit(TEq, lambda e, dc=dc, par=par, b4=b4: e.transpose(
                            pb[b4][:, dc * 128:dc * 128 + 128],
                            ffx[par][:, (dc + 4) * 128:(dc + 4) * 128 + 128],
                            ident[:]), tte)
                    w(SEq, tte)
                    emit(SEq, lambda e, ic=ic, b3=b3: e.copy(
                        fxnth4[:, ic * 1024:ic * 1024 + 512], pb[b3][:]), tse)
                    emit(SEq, lambda e, ic=ic, b4=b4: e.copy(
                        fxnth4[:, ic * 1024 + 512:ic * 1024 + 1024], pb[b4][:]),
                        tse)
                    w(VEq, tse)
                    emit(VEq, lambda e, ic=ic, b3=b3: e.tensor_sub(
                        fxntl4[:, ic * 1024:ic * 1024 + 512], pb[b3][:],
                        fxnth4[:, ic * 1024:ic * 1024 + 512]), tve)
                    emit(VEq, lambda e, ic=ic, b4=b4: e.tensor_sub(
                        fxntl4[:, ic * 1024 + 512:ic * 1024 + 1024], pb[b4][:],
                        fxnth4[:, ic * 1024 + 512:ic * 1024 + 1024]), tve)
                    trot.rel(b3, tve)
                    trot.rel(b4, tve)
                    ve_join()
                    if fixed:
                        emit(VEq, lambda e, ent=ent, ic=ic, par=par: e.tensor_scalar(
                            mcol4[:, ic:ic + 1], ffx[par][:, 1024:1025],
                            e1col[:, ent:ent + 1], None, ALU.is_equal), tve)
                    else:
                        emit(VEq, lambda e, ee=ee, ic=ic, par=par:
                             e.tensor_single_scalar(
                                 mcol4[:, ic:ic + 1], ffx[par][:, 1024:1025],
                                 float(ee), ALU.is_equal), tve)
                        ve_join()
                        emit(VEq, lambda e, ee=ee, ic=ic: e.tensor_mul(
                            mcol4[:, ic:ic + 1], mcol4[:, ic:ic + 1],
                            flagcol[:, ee:ee + 1]), tve)
                    ffx_free[par] = (tve, tve.n)
                    ffx_free_te[par] = (tte, tte.n)
                fx_ready = tve.n
                for half, (buf, wld) in enumerate((wl0, wl1)):
                    first_w = True
                    for ic in range(4):
                        for fg in range(2):
                            mb = rot.acq()
                            if first_w:
                                w(TEq, dfw, wld)
                                w(TEq, tve, fx_ready)
                                first_w = False
                            for hcl in range(4):
                                for pi, (xt_, wo_) in enumerate(
                                        ((fxnth4, 0), (fxnth4, 4096),
                                         (fxntl4, 0))):
                                    emit(TEq, lambda e, ic=ic, fg=fg, hcl=hcl,
                                         buf=buf, xt_=xt_, wo_=wo_, pi=pi,
                                         half=half, mb=mb: e.matmul(
                                             pb[mb][:],
                                             xt_[:, ic * 1024 + (half * 4 + hcl) * 128:
                                                 ic * 1024 + (half * 4 + hcl) * 128 + 128],
                                             pbB[:, buf * 8192 + wo_ + hcl * 1024 + fg * 512:
                                                 buf * 8192 + wo_ + hcl * 1024 + fg * 512 + 512],
                                             start=(hcl == 0 and pi == 0),
                                             stop=(hcl == 3 and pi == 2)),
                                         tte if (hcl == 3 and pi == 2) else None)
                            if half == 0:
                                w(SEq, tte)
                                if ffy_free[ent % 2] is not None:
                                    fs, fv = ffy_free[ent % 2]
                                    w(SEq, fs, fv)
                                emit(SEq, lambda e, ic=ic, fg=fg, mb=mb,
                                     ffy4=ffy4: e.copy(
                                         ffy4[:, ic * 1024 + fg * 512:
                                              ic * 1024 + fg * 512 + 512],
                                         pb[mb][:]), tse)
                                rot.rel(mb, tse)
                            else:
                                w(VEq, tte)
                                ve_join()
                                w(VEq, tse)
                                emit(VEq, lambda e, ic=ic, fg=fg, mb=mb,
                                     ffy4=ffy4: e.tensor_add(
                                         ffy4[:, ic * 1024 + fg * 512:
                                              ic * 1024 + fg * 512 + 512],
                                         ffy4[:, ic * 1024 + fg * 512:
                                              ic * 1024 + fg * 512 + 512],
                                         pb[mb][:]), tve)
                                rot.rel(mb, tve)
                pbq_free[wl0[0]] = (tte, tte.n)
                pbq_free[wl1[0]] = (tte, tte.n)
                for ic in range(4):
                    ve_join()
                    emit(VEq, lambda e, ic=ic, ffy4=ffy4: e.tensor_scalar(
                        ffy4[:, ic * 1024:ic * 1024 + 1024],
                        ffy4[:, ic * 1024:ic * 1024 + 1024],
                        mcol4[:, ic:ic + 1], None, ALU.mult), tve)
                    w(GPq, tve)
                    if fixed:
                        emit(GPq, lambda e, ent=ent, ic=ic, ffy4=ffy4: e.dma_start(
                            out=scr_y[ent * 512 + ic * 128:
                                      ent * 512 + ic * 128 + 128, :],
                            in_=ffy4[:, ic * 1024:ic * 1024 + 1024]), dyw, 16)
                    else:
                        w(GPq, dyw)
                        emit(GPq, lambda e, ee=ee, ic=ic, ffy4=ffy4:
                             e.indirect_dma_start(
                                 out=scr_y,
                                 out_offset=IndirectOffsetOnAxis(
                                     xidx[:, (ee - 1) * 4 + ic:
                                          (ee - 1) * 4 + ic + 1], 0),
                                 in_=ffy4[:, ic * 1024:ic * 1024 + 1024],
                                 in_offset=None, compute_op=ALU.add), dyw, 16)
                dyw_hist[0] = dyw_hist[1]
                dyw_hist[1] = dyw.n
                ffy_free[ent % 2] = (dyw, dyw.n)

        out_dmas = 0
        kvstack_ready_l = {}
        for seq in range(SEQS):
            for _b in range(3):
                kv_first[(_b, 0)] = True
                kv_first[(_b, 1)] = True
            for sti in range(NST):
                c0 = seq * CPS + sti * CST
                if (seq, sti) != (0, 0):
                    w(VEq, daw)
                    w(SEq, dxw)
                for i in range(CST):
                    plan_chunk_A(c0 + i, i, None, seq, sti)
                for g in range(2):
                    plan_kv_part(g, 0, c0, seq)
                    plan_kv_part(g, 1, c0, seq)
                    plan_kv_einsum(g, seq, sti == NST - 1)
                plan_gate_masks(c0)
                w(GPq, tve)
                emit(GPq, lambda e, c0=c0: e.dma_start(
                    out=scr_acc[c0:c0 + CST].rearrange("c p f -> p c f"),
                    in_=acc_st[:]), daw, 16)
            w(VEq, tte)
            emit(VEq, lambda e, seq=seq: e.tensor_copy(
                kvstack[:, seq * 1040:seq * 1040 + 455], pb[5][:, 0:455]), tve)
            emit(VEq, lambda e, seq=seq: e.tensor_copy(
                kvstack[:, seq * 1040 + 455:seq * 1040 + 910],
                pb[6][:, 0:455]), tve)
            emit(VEq, lambda e, seq=seq: e.tensor_copy(
                kvstack[:, seq * 1040 + 910:seq * 1040 + 1040],
                pb[7][:, 0:130]), tve)
            kvstack_ready_l[seq] = tve.n

        plan_bookkeeping()
        plan_ffn()
        ffn_done = dyw.n
        w(SYq, tte)
        emit(SYq, lambda e: e.dma_start(
            out=poolB[:],
            in_=wout.rearrange("(c p) f -> p c f", p=128)), dwe, 16)
        wout_ready = dwe.n

        for seq in range(SEQS):
            kvstack_ready = kvstack_ready_l[seq]
            if True:
                for tg in range(4):
                    cg0 = seq * CPS + tg * 4
                    w(GPq, tte)
                    w(GPq, dxw)
                    emit(GPq, lambda e, cg0=cg0: e.dma_start(
                        out=xnth[:, 0:4096],
                        in_=scr_xnth[cg0:cg0 + 4].rearrange("c p f -> p c f")),
                        dxr, 16)
                    emit(GPq, lambda e, cg0=cg0: e.dma_start(
                        out=xntl[:, 0:4096],
                        in_=scr_xntl[cg0:cg0 + 4].rearrange("c p f -> p c f")),
                        dxr, 16)
                    w(GPq, daw)
                    w(GPq, dout)
                    emit(GPq, lambda e, cg0=cg0: e.dma_start(
                        out=acc_st[:, 0:4096],
                        in_=scr_acc[cg0:cg0 + 4].rearrange("c p f -> p c f")),
                        dar, 16)
                    xnt2_ready = dxr.n
                    acc2_ready = dar.n
                    for qc in range(8):
                        parW = wst_ctr[0] % 2
                        wst_ctr[0] += 1
                        if wst_free[parW] is not None:
                            fs, fv = wst_free[parW]
                            w(SYq, fs, fv)
                        emit(SYq, lambda e, qc=qc, parW=parW: e.dma_start(
                            out=wsth[:, parW * 1024:parW * 1024 + 1024],
                            in_=scr_wqh.rearrange("(dc p) q -> p dc q", p=128)[
                                :, :, qc * 128:qc * 128 + 128]), dwst, 16)
                        emit(SYq, lambda e, qc=qc, parW=parW: e.dma_start(
                            out=wstl[:, parW * 1024:parW * 1024 + 1024],
                            in_=scr_wql.rearrange("(dc p) q -> p dc q", p=128)[
                                :, :, qc * 128:qc * 128 + 128]), dwst, 16)
                        wload = dwst.n
                        mb = rot.acq()
                        w(TEq, dwst, wload)
                        w(TEq, dxr, xnt2_ready)
                        for dc in range(8):
                            for pi, (wt, xt_) in enumerate(
                                    ((wsth, xnth), (wsth, xntl), (wstl, xnth))):
                                emit(TEq, lambda e, dc=dc, parW=parW, mb=mb,
                                     wt=wt, xt_=xt_, pi=pi: e.matmul(
                                         pb[mb][:],
                                         wt[:, parW * 1024 + dc * 128:
                                            parW * 1024 + dc * 128 + 128],
                                         xnt_ap(xt_, dc, 4, 0),
                                         start=(dc == 0 and pi == 0),
                                         stop=(dc == 7 and pi == 2)),
                                     tte if (dc == 7 and pi == 2) else None)
                        w(SEq, tte)
                        emit(SEq, lambda e, qc=qc, mb=mb: e.activation(
                            qrelT[:, qc * 512:qc * 512 + 512], pb[mb][:], AF.Relu,
                            bias=bqkv_col[:, qc:qc + 1]), tse)
                        rot.rel(mb, tse)
                        wst_free[parW] = (tte, tte.n)
                    qrelT_ready = tse.n
                    for ic in range(4):
                        posidx = tg * 4 + ic
                        w(TEq, tse, qrelT_ready)
                        tb3 = trot.acq()
                        for qc in range(4):
                            emit(TEq, lambda e, qc=qc, ic=ic, tb3=tb3: e.transpose(
                                pb[tb3][:, qc * 128:qc * 128 + 128],
                                qrelT[:, qc * 512 + ic * 128:qc * 512 + ic * 128 + 128],
                                ident[:]), None)
                        tb4 = trot.acq()
                        for qc in range(4):
                            emit(TEq, lambda e, qc=qc, ic=ic, tb4=tb4: e.transpose(
                                pb[tb4][:, qc * 128:qc * 128 + 128],
                                qrelT[:, (qc + 4) * 512 + ic * 128:
                                      (qc + 4) * 512 + ic * 128 + 128],
                                ident[:]), tte)
                        w(VEq, tte)
                        emit(VEq, lambda e, tb3=tb3: e.tensor_copy(
                            qrel_tok[:, 0:512], pb[tb3][:]), tve)
                        emit(VEq, lambda e, tb4=tb4: e.tensor_copy(
                            qrel_tok[:, 512:1024], pb[tb4][:]), tve)
                        trot.rel(tb3, tve)
                        trot.rel(tb4, tve)
                        ve_join()
                        for trig in range(2):
                            tab = costab if trig == 0 else sintab
                            emit(VEq, lambda e, trig=trig, tab=tab, posidx=posidx:
                                 e.tensor_scalar(
                                     qsc.rearrange(
                                         "p (h j) -> p h j", j=128)[
                                         :, :, trig * 64:trig * 64 + 64],
                                     qrel_tok[:, 0:1024].rearrange(
                                         "p (h j) -> p h j", j=64),
                                     tab[:, posidx:posidx + 1], None, ALU.mult),
                                 tve)
                        qsc_ready = tve.n
                        w(TEq, tve, qsc_ready)
                        for grp in range(4):
                            tb = trot.acq()
                            for k2 in range(4):
                                hh = grp * 4 + k2
                                emit(TEq, lambda e, hh=hh, k2=k2, tb=tb: e.transpose(
                                    pb[tb][:, k2 * 128:k2 * 128 + 128],
                                    qsc[:, hh * 128:hh * 128 + 128],
                                    ident[:]), tte if k2 == 3 else None)
                            w(VEq, tte)
                            emit(VEq, lambda e, grp=grp, tb=tb: e.tensor_copy(
                                qstack[:, grp * 512:grp * 512 + 512], pb[tb][:]),
                                tve)
                            trot.rel(tb, tve)
                        qstack_ready = tve.n
                        w(TEq, tve, qstack_ready)
                        w(TEq, tve, kvstack_ready)
                        ctx_banks = []
                        for h in range(16):
                            bk = h // 7
                            if bk >= len(ctx_banks):
                                ctx_banks.append(rot.acq())
                            mb = ctx_banks[bk]
                            co = (h % 7) * 65
                            emit(TEq, lambda e, h=h, mb=mb, co=co, seq=seq:
                                 e.matmul(
                                     pb[mb][:, co:co + 65],
                                     qstack[:, h * 128:h * 128 + 128],
                                     kvstack[:, seq * 1040 + h * 65:
                                             seq * 1040 + h * 65 + 65],
                                     start=True, stop=True),
                                 tte if h == 15 else None)
                        w(VEq, tte)
                        for bi, nh in ((0, 7), (1, 7), (2, 2)):
                            mb = ctx_banks[bi]
                            emit(VEq, lambda e, mb=mb, bi=bi, nh=nh:
                                 e.tensor_single_scalar(
                                     rec16[:, bi * 7:bi * 7 + nh].rearrange(
                                         "p (h j) -> p h j", j=1),
                                     pb[mb][:, 0:nh * 65].rearrange(
                                         "p (h j) -> p h j", j=65)[:, :, 64:65],
                                     EPS_ATT, ALU.add), tve)
                        ve_join()
                        emit(VEq, lambda e: e.reciprocal(rec16b[:], rec16[:]), tve)
                        rec_ready = tve.n
                        w(SEq, tve, rec_ready)
                        for h in range(16):
                            mb = ctx_banks[h // 7]
                            co = (h % 7) * 65
                            emit(SEq, lambda e, h=h, mb=mb, co=co: e.activation(
                                attn[:, h * 64:h * 64 + 64], pb[mb][:, co:co + 64],
                                AF.Identity, scale=rec16b[:, h:h + 1]), tse)
                        for mb in ctx_banks:
                            rot.rel(mb, tse)
                        attn_ready = tse.n
                        w(TEq, tse, attn_ready)
                        tb3 = trot.acq()
                        for ec in range(4):
                            emit(TEq, lambda e, ec=ec, tb3=tb3: e.transpose(
                                pb[tb3][:, ec * 128:ec * 128 + 128],
                                attn[:, ec * 128:ec * 128 + 128], ident[:]), None)
                        tb4 = trot.acq()
                        for ec in range(4):
                            emit(TEq, lambda e, ec=ec, tb4=tb4: e.transpose(
                                pb[tb4][:, ec * 128:ec * 128 + 128],
                                attn[:, (ec + 4) * 128:(ec + 4) * 128 + 128],
                                ident[:]), tte)
                        w(VEq, tte)
                        emit(VEq, lambda e, tb3=tb3: e.tensor_copy(
                            attnT[:, 0:512], pb[tb3][:]), tve)
                        emit(VEq, lambda e, tb4=tb4: e.tensor_copy(
                            attnT[:, 512:1024], pb[tb4][:]), tve)
                        trot.rel(tb3, tve)
                        trot.rel(tb4, tve)
                        attnT_ready = tve.n
                        for fg in range(2):
                            mb = rot.acq()
                            w(TEq, tve, attnT_ready)
                            if seq == 0 and tg == 0 and ic == 0 and fg == 0:
                                w(TEq, dwe, wout_ready)
                            for ec in range(8):
                                emit(TEq, lambda e, ec=ec, fg=fg, mb=mb: e.matmul(
                                    pb[mb][:],
                                    attnT[:, ec * 128:ec * 128 + 128],
                                    poolB[:, ec * 1024 + fg * 512:
                                          ec * 1024 + fg * 512 + 512],
                                    start=(ec == 0), stop=False), None)
                            emit(TEq, lambda e, fg=fg, mb=mb: e.matmul(
                                pb[mb][:], ones1[:],
                                bout_row[0:1, fg * 512:fg * 512 + 512],
                                start=False, stop=True), tte)
                            w(VEq, tte)
                            w(VEq, dar, acc2_ready)
                            ve_join()
                            emit(VEq, lambda e, ic=ic, fg=fg, mb=mb: e.tensor_add(
                                acc_st[:, ic * 1024 + fg * 512:
                                       ic * 1024 + fg * 512 + 512],
                                acc_st[:, ic * 1024 + fg * 512:
                                       ic * 1024 + fg * 512 + 512],
                                pb[mb][:]), tve)
                            rot.rel(mb, tve)
                        ygat = poolA[:, 11264:12288]
                        w(GPq, tve)
                        w(GPq, dyw, ffn_done)
                        emit(GPq, lambda e, cg0=cg0, ic=ic, ygat=ygat:
                             e.indirect_dma_start(
                                 out=ygat, out_offset=None,
                                 in_=scr_y,
                                 in_offset=IndirectOffsetOnAxis(
                                     pos_u32[:, cg0 + ic:cg0 + ic + 1], 0)),
                             dyr, 16)
                        ygld = dyr.n
                        w(VEq, dyr, ygld)
                        ve_join()
                        emit(VEq, lambda e, ic=ic, ygat=ygat: e.tensor_add(
                            acc_st[:, ic * 1024:ic * 1024 + 1024],
                            acc_st[:, ic * 1024:ic * 1024 + 1024], ygat), tve)
                    w(GPq, tve)
                    w(GPq, dyr)
                    emit(GPq, lambda e, cg0=cg0: e.dma_start(
                        out=out_t[cg0 * 128:cg0 * 128 + 512, :].rearrange(
                            "(c p) f -> p c f", p=128),
                        in_=acc_st[:, 0:4096]), dout, 16)
                    out_dmas += 1

        if STAGE < 4:
            w(GPq, daw)
            for c in range(32):
                emit(GPq, lambda e, c=c: e.dma_start(
                    out=out_t[c * 128:(c + 1) * 128, :], in_=scr_acc[c]), dout, 16)
                out_dmas += 1

        if DBG:
            w(GPq, tve)
            emit(GPq, lambda e: e.dma_start(out=o_pos[:], in_=pos_u32[:]), dout, 16)
            emit(GPq, lambda e: e.dma_start(out=o_chosen[:], in_=chosen_all[:]),
                 dout, 16)
            emit(GPq, lambda e: e.dma_start(out=o_rows[0:1, 0:256], in_=cnt_row[:]),
                 dout, 16)
            emit(GPq, lambda e: e.dma_start(out=o_rows[0:1, 256:512], in_=coffx[:]),
                 dout, 16)
            emit(GPq, lambda e: e.dma_start(
                out=o_rows[0:1, 512:520], in_=cumtot[:]), dout, 16)
            emit(GPq, lambda e: e.dma_start(
                out=o_rows[0:1, 520:528], in_=basex[:]), dout, 16)
            emit(GPq, lambda e: e.dma_start(
                out=o_rows[0:1, 528:536], in_=e1row[:]), dout, 16)
            emit(GPq, lambda e: e.dma_start(
                out=o_rows[0:1, 536:544], in_=tbrow[:]), dout, 16)
            emit(GPq, lambda e: e.dma_start(
                out=o_rows[0:1, 544:552], in_=flagrow[:]), dout, 16)
            emit(GPq, lambda e: e.dma_start(out=o_kvs[:], in_=kvstack[:]), dout, 16)
            out_dmas += 1
            w(GPq, dyw)
            for cc in range(32):
                emit(GPq, lambda e, cc=cc: e.dma_start(
                    out=o_y[cc * 128:(cc + 1) * 128, :], in_=scr_y[cc * 128:(cc + 1) * 128, :]), dout, 16)
                emit(GPq, lambda e, cc=cc: e.dma_start(
                    out=o_xs[cc * 128:(cc + 1) * 128, :], in_=scr_xsort[cc * 128:(cc + 1) * 128, :]), dout, 16)
            out_dmas += 9 + 64
        w(GPq, dout, out_dmas * 16)

        block = es.enter_context(nc.Block())

        @block.sync
        def _(e):
            for fn in SYq:
                fn(e)

        @block.tensor
        def _(e):
            for fn in TEq:
                fn(e)

        @block.vector
        def _(e):
            for fn in VEq:
                fn(e)

        @block.scalar
        def _(e):
            for fn in SEq:
                fn(e)

        @block.gpsimd
        def _(e):
            for fn in GPq:
                fn(e)

    return nc


_NC_CACHE = None


def kernel(L0_attn_x, L0_attn_W_qkv, L0_attn_b_qkv, L0_attn_W_out, L0_attn_b_out,
           L0_ffn_gate_w, L0_ffn_expert_w):
    global _NC_CACHE
    x = np.ascontiguousarray(np.asarray(L0_attn_x, dtype=np.float32))
    wq = np.ascontiguousarray(np.asarray(L0_attn_W_qkv, dtype=np.float32))
    bq = np.ascontiguousarray(np.asarray(L0_attn_b_qkv, dtype=np.float32)).reshape(1, 3 * E)
    wo = np.ascontiguousarray(np.asarray(L0_attn_W_out, dtype=np.float32))
    bo = np.ascontiguousarray(np.asarray(L0_attn_b_out, dtype=np.float32)).reshape(1, E)
    gw = np.ascontiguousarray(np.asarray(L0_ffn_gate_w, dtype=np.float32))
    ew = np.ascontiguousarray(np.asarray(L0_ffn_expert_w, dtype=np.float32))
    if _NC_CACHE is None:
        _NC_CACHE = build_nc()
    nc = _NC_CACHE
    in_maps = []
    for c in range(NCORE):
        xc = np.ascontiguousarray(x[2 * c:2 * c + 2].reshape(T, E))
        in_maps.append({"x": xc, "wqkv": wq, "bqkv": bq, "wout": wo, "bout": bo,
                        "gate": gw, "expw": ew})
    res = run_bass_kernel_spmd(nc, in_maps, core_ids=list(range(NCORE)))
    out = np.stack([np.asarray(res.results[c]["out"]) for c in range(NCORE)])
    return np.ascontiguousarray(out.reshape(B, L, E))


# version 21
# speedup vs baseline: 1.0288x; 1.0288x over previous
import contextlib
import numpy as np
import concourse.bass as bass
import concourse.mybir as mybir
from concourse.bass_utils import run_bass_kernel_spmd

F32 = mybir.dt.float32
F32R = mybir.dt.float32r
BF16 = mybir.dt.bfloat16
I32 = mybir.dt.int32
AF = mybir.ActivationFunctionType
ALU = mybir.AluOpType
AX = mybir.AxisListType

B, L, E, H, HD = 16, 2048, 1024, 16, 64
NCORE = 8
SEQS = 2
T = SEQS * L
CPS = L // 128
NST = 2
CST = 8
NE = 8
EPS_LN = 1e-5
EPS_ATT = 1e-4
PI = float(np.pi)

STAGE = 4
DBG = 0


class Sem:
    __slots__ = ("h", "n")

    def __init__(self, h):
        self.h = h
        self.n = 0

    def inc(self, k=1):
        self.n += k
        return self.n


def build_nc():
    nc = bass.Bass("TRN2", target_bir_lowering=False, debug=False)
    x_in = nc.dram_tensor("x", [T, E], F32, kind="ExternalInput").ap()
    wqkv = nc.dram_tensor("wqkv", [E, 3 * E], F32, kind="ExternalInput").ap()
    bqkv = nc.dram_tensor("bqkv", [1, 3 * E], F32, kind="ExternalInput").ap()
    wout = nc.dram_tensor("wout", [E, E], F32, kind="ExternalInput").ap()
    bout = nc.dram_tensor("bout", [1, E], F32, kind="ExternalInput").ap()
    gate = nc.dram_tensor("gate", [NE, E], F32, kind="ExternalInput").ap()
    expw = nc.dram_tensor("expw", [NE, E, E], F32, kind="ExternalInput").ap()
    out_t = nc.dram_tensor("out", [T, E], F32, kind="ExternalOutput").ap()
    scr_weth = nc.dram_tensor("scr_weth", [NE, 128, 8, E], BF16).ap()
    scr_wetl = nc.dram_tensor("scr_wetl", [NE, 128, 8, E], BF16).ap()
    scr_xtok = nc.dram_tensor("scr_xtok", [32, 128, E], F32).ap()
    scr_xsort = nc.dram_tensor("scr_xsort", [T, 1088], F32).ap()
    scr_y = nc.dram_tensor("scr_y", [T, E], F32).ap()
    if DBG:
        o_pos = nc.dram_tensor("o_pos", [128, 32], mybir.dt.uint32,
                               kind="ExternalOutput").ap()
        o_chosen = nc.dram_tensor("o_chosen", [128, 32], F32,
                                  kind="ExternalOutput").ap()
        o_rows = nc.dram_tensor("o_rows", [1, 256 * 3 + 8 * 5], F32,
                                kind="ExternalOutput").ap()
        o_y = nc.dram_tensor("o_y", [T, E], F32, kind="ExternalOutput").ap()
        o_xs = nc.dram_tensor("o_xs", [T, 1088], F32, kind="ExternalOutput").ap()
        o_kvs = nc.dram_tensor("o_kvs", [128, 2080], F32, kind="ExternalOutput").ap()
    scr_wqh = nc.dram_tensor("scr_wqh", [E, 3 * E], BF16).ap()
    scr_wql = nc.dram_tensor("scr_wql", [E, 3 * E], BF16).ap()
    scr_xnth = nc.dram_tensor("scr_xnth", [32, 128, E], BF16).ap()
    scr_xntl = nc.dram_tensor("scr_xntl", [32, 128, E], BF16).ap()
    scr_acc = nc.dram_tensor("scr_acc", [32, 128, E], F32).ap()

    es = contextlib.ExitStack()
    with es:
        sb = lambda nm, shp, dt=F32: es.enter_context(nc.sbuf_tensor(nm, shp, dt))
        xt2 = sb("xt2", [128, 2048])
        xnc2 = sb("xnc2", [128, 1024])
        xnth = sb("xnth", [128, 8192], BF16)
        xntl = sb("xntl", [128, 8192], BF16)
        acc_st = sb("acc_st", [128, 8192])
        wsth = sb("wsth", [128, 2048], BF16)
        wstl = sb("wstl", [128, 2048], BF16)
        kvt4 = sb("kvt4", [128, 4096])
        poolA = sb("poolA", [128, 12416])
        poolB = sb("poolB", [128, 8192])
        kvstack = sb("kvstack", [128, 2080])
        ident = sb("ident", [128, 128])
        posf = sb("posf", [128, 16])
        posi = sb("posi", [128, 16], I32)
        sintab = sb("sintab", [128, 16])
        costab = sb("costab", [128, 16])
        stats = sb("stats", [128, 12])
        mvt = sb("mvt", [128, 2])
        sq1 = sb("sq1", [128, 1])
        rstd = sb("rstd", [128, 1])
        epsc = sb("epsc", [128, 1])
        pih = sb("pih", [128, 1])
        ones1 = sb("ones1", [1, 128])
        bqkv_col = sb("bqkv_col", [128, 24])
        bout_row = sb("bout_row", [1, E])
        gt_in = None
        gwt = sb("gwt", [128, 64])
        gwth = sb("gwth", [128, 64], BF16)
        gwtl = sb("gwtl", [128, 64], BF16)
        wets2h = sb("wets2h", [128, 2048], BF16)
        wets2l = sb("wets2l", [128, 2048], BF16)
        masks_st = sb("masks_st", [128, 256], BF16)
        chosen_all = sb("chosen_all", [128, 32])
        pos_u32 = sb("pos_u32", [128, 32], mybir.dt.uint32)
        iota8f = sb("iota8f", [128, 8])
        iotacol = sb("iotacol", [128, 1])
        iotac4 = sb("iotac4", [128, 4])
        ustrict = sb("ustrict", [128, 128], BF16)
        ustri = sb("ustri", [128, 128], I32)
        onescol = sb("onescol", [128, 1], BF16)
        cnt_row = sb("cnt_row", [1, 256])
        coffx = sb("coffx", [1, 256])
        combo_row = cnt_row
        cumtot = sb("cumtot", [1, 8])
        basex = sb("basex", [1, 8])
        e1row = sb("e1row", [1, 8])
        tbrow = sb("tbrow", [1, 8])
        flagrow = sb("flagrow", [1, 8])
        tmprow = sb("tmprow", [1, 8])
        tmprow2 = sb("tmprow2", [1, 8])
        zrow = sb("zrow", [1, 32])
        e1col = sb("e1col", [128, 8])
        tbcol = sb("tbcol", [128, 8])
        flagcol = sb("flagcol", [128, 8])
        widx = sb("widx", [128, 16], mybir.dt.uint32)
        io2 = sb("io2", [128, 1])
        io2i = sb("io2i", [128, 1], I32)
        xidx = sb("xidx", [128, 28], mybir.dt.uint32)
        xidxf = sb("xidxf", [128, 28])
        tmpcol8 = sb("tmpcol8", [128, 8])
        mx1 = sb("mx1", [128, 1])
        m08 = sb("m08", [128, 8])
        cum8 = sb("cum8", [128, 8])
        meq8 = sb("meq8", [128, 8])
        zero8 = sb("zero8", [128, 8])
        rec16 = sb("rec16", [128, 16])
        rec16b = sb("rec16b", [128, 16])

        pb = [es.enter_context(nc.psum_tensor(f"pb{i}", [128, 512], F32))
              for i in range(8)]

        mksem = lambda nm: Sem(es.enter_context(nc.semaphore(nm)))
        dx = mksem("dx")
        dwst = mksem("dwst")
        dwe = mksem("dwe")
        dmisc = mksem("dmisc")
        dscw = mksem("dscw")
        dxw = mksem("dxw")
        dxr = mksem("dxr")
        daw = mksem("daw")
        dar = mksem("dar")
        dout = mksem("dout")
        dxtw = mksem("dxtw")
        dsc = mksem("dsc")
        dyw = mksem("dyw")
        dfx = mksem("dfx")
        dfw = mksem("dfw")
        dyr = mksem("dyr")
        tte = mksem("tte")
        tve = mksem("tve")
        tse = mksem("tse")
        tgp = mksem("tgp")

        TEq, VEq, SEq, GPq, SYq = [], [], [], [], []

        def w(q, s, v=None):
            vv = s.n if v is None else v
            q.append(lambda e, s=s, vv=vv: e.wait_ge(s.h, vv))

        def emit(q, fn, s=None, k=1):
            if s is None:
                q.append(fn)
            else:
                s.inc(k)
                q.append(lambda e, fn=fn, s=s, k=k: fn(e).then_inc(s.h, k))

        class Rot:
            def __init__(self):
                self.i = 0
                self.free = [None, None, None]

            def acq(self):
                b = self.i % 3
                self.i += 1
                fr = self.free[b]
                if fr is not None:
                    w(TEq, fr[0], fr[1])
                return b

            def rel(self, b, sem):
                self.free[b] = (sem, sem.n)

        rot = Rot()
        class TRot:
            def __init__(self):
                self.i = 0
                self.free = [None, None]

            def acq(self):
                b = self.i % 2
                self.i += 1
                fr = self.free[b]
                if fr is not None:
                    w(TEq, fr[0], fr[1])
                return 3 + b

            def rel(self, b, sem):
                self.free[b - 3] = (sem, sem.n)

        trot = TRot()

        def ve_join():
            w(VEq, tve)

        def se_join():
            w(SEq, tse)

        def bar():
            w(TEq, tve)
            w(TEq, tse)
            w(VEq, tte)
            w(VEq, tse)
            w(SEq, tte)
            w(SEq, tve)

        gt_in = xt2[0:8, 0:1024]
        emit(SYq, lambda e: e.dma_start(out=gt_in, in_=gate[:]), dmisc, 16)
        emit(SYq, lambda e: e.dma_start(
            out=bqkv_col[:], in_=bqkv.rearrange("o (c p) -> (o p) c", p=128),
            allow_slow_non_contiguous=True),
            dmisc, 16)
        emit(SYq, lambda e: e.dma_start(out=bout_row[:], in_=bout[:]), dmisc, 16)

        emit(GPq, lambda e: e.iota(ident[:].bitcast(I32), pattern=[[-1, 128]],
                                   base=0, channel_multiplier=1))
        emit(GPq, lambda e: e.iota(posi[:], pattern=[[128, 16]],
                                   base=0, channel_multiplier=1))
        emit(GPq, lambda e: e.iota(iota8f[:].bitcast(I32), pattern=[[1, 8]],
                                   base=0, channel_multiplier=0))
        emit(GPq, lambda e: e.iota(iotacol[:].bitcast(I32), pattern=[[1, 1]],
                                   base=0, channel_multiplier=1))
        emit(GPq, lambda e: e.iota(io2i[:], pattern=[[1, 1]],
                                   base=0, channel_multiplier=2))
        emit(GPq, lambda e: e.iota(iotac4[:].bitcast(I32), pattern=[[128, 4]],
                                   base=0, channel_multiplier=1))
        emit(GPq, lambda e: e.iota(ustri[:], pattern=[[-1, 128]],
                                   base=0, channel_multiplier=1))
        emit(GPq, lambda e: e.memset(onescol[:], 1.0))
        emit(GPq, lambda e: e.memset(zrow[:], 0.0))
        emit(GPq, lambda e: e.memset(ones1[:], 1.0))
        emit(GPq, lambda e: e.memset(epsc[:], EPS_LN))
        emit(GPq, lambda e: e.memset(pih[:], PI / 2))
        emit(GPq, lambda e: e.memset(zero8[:], 0.0), tgp)

        w(VEq, tgp)
        emit(VEq, lambda e: e.tensor_single_scalar(
            ident[:], ident[:].bitcast(I32), 0, ALU.is_equal), tve)
        emit(VEq, lambda e: e.tensor_single_scalar(
            ustrict[:], ustri[:], 0, ALU.is_lt), tve)
        emit(VEq, lambda e: e.tensor_copy(iota8f[:], iota8f[:].bitcast(I32)), tve)
        emit(VEq, lambda e: e.tensor_copy(iotacol[:], iotacol[:].bitcast(I32)), tve)
        emit(VEq, lambda e: e.tensor_copy(io2[:], io2i[:]), tve)
        emit(VEq, lambda e: e.tensor_copy(iotac4[:], iotac4[:].bitcast(I32)), tve)
        ve_join()
        emit(VEq, lambda e: e.tensor_copy(posf[:], posi[:]), tve)

        w(SEq, tve)
        emit(SEq, lambda e: e.activation(sintab[:], posf[:], AF.Sin,
                                         scale=PI / (2.0 * L)), tse)
        emit(SEq, lambda e: e.activation(costab[:], posf[:], AF.Sin,
                                         bias=pih[:], scale=-PI / (2.0 * L)), tse)

        w(TEq, tve)
        w(TEq, dmisc, 16)
        for dc in range(8):
            emit(TEq, lambda e, dc=dc: e.transpose(
                pb[3][:, dc * 8:dc * 8 + 8], gt_in[:, dc * 128:dc * 128 + 128],
                ident[0:8, 0:8]), tte if dc == 7 else None)
        gate_tr_done = tte.n
        w(VEq, tte)
        emit(VEq, lambda e: e.tensor_copy(gwt[:], pb[3][:, 0:64]), tve)
        w(SEq, tve)
        emit(SEq, lambda e: e.copy(gwth[:], gwt[:]), tse)
        w(VEq, tse)
        emit(VEq, lambda e: e.tensor_sub(gwtl[:], gwt[:], gwth[:]), tve)
        gwt_ready = tve.n
        gwt_ready_se = tse.n

        pAb = poolA[:].bitcast(BF16)
        wsplit = poolA[:, 8192:11264]
        wsp_free = [None, None]
        for wc in range(8):
            par = wc % 2
            emit(SYq, lambda e, wc=wc: e.dma_start(
                out=wsplit[:], in_=wqkv[wc * 128:wc * 128 + 128, :]), dmisc, 16)
            wld = dmisc.n
            hi = pAb[:, par * 6144:par * 6144 + 3072]
            lo = pAb[:, par * 6144 + 3072:par * 6144 + 6144]
            w(SEq, dmisc, wld)
            if wsp_free[par] is not None:
                fs, fv = wsp_free[par]
                w(SEq, fs, fv)
            emit(SEq, lambda e, hi=hi: e.copy(hi, wsplit[:]), tse)
            w(VEq, tse)
            emit(VEq, lambda e, hi=hi, lo=lo: e.tensor_sub(
                lo, wsplit[:], hi), tve)
            w(SYq, tve)
            w(GPq, tve)
            emit(GPq, lambda e, wc=wc, hi=hi: e.dma_start(
                out=scr_wqh[wc * 128:wc * 128 + 128, :], in_=hi), dscw, 16)
            emit(GPq, lambda e, wc=wc, lo=lo: e.dma_start(
                out=scr_wql[wc * 128:wc * 128 + 128, :], in_=lo), dscw, 16)
            wsp_free[par] = (dscw, dscw.n)

        def plan_expert_prep():
            wets_h = [wets2h[:, 0:1024], wets2h[:, 1024:2048]]
            wets_l = [wets2l[:, 0:1024], wets2l[:, 1024:2048]]
            wets_free = [None, None]
            for ei in range(NE):
                if ei > 0:
                    w(SYq, tte)
                emit(SYq, lambda e, ei=ei: e.dma_start(
                    out=poolB[:],
                    in_=expw[ei].rearrange("(c p) h -> p c h", p=128)),
                    dwe, 16)
                w(TEq, dwe)
                for hc in range(8):
                    par = (ei * 8 + hc) % 2
                    if wets_free[par] is not None:
                        fs, fv = wets_free[par]
                    tb3 = trot.acq()
                    for fc in range(4):
                        emit(TEq, lambda e, fc=fc, hc=hc, tb3=tb3: e.transpose(
                            pb[tb3][:, fc * 128:fc * 128 + 128],
                            poolB[:, fc * 1024 + hc * 128:fc * 1024 + hc * 128 + 128],
                            ident[:]), None)
                    tb4 = trot.acq()
                    for fc in range(4):
                        emit(TEq, lambda e, fc=fc, hc=hc, tb4=tb4: e.transpose(
                            pb[tb4][:, fc * 128:fc * 128 + 128],
                            poolB[:, (fc + 4) * 1024 + hc * 128:(fc + 4) * 1024 + hc * 128 + 128],
                            ident[:]), tte)
                    w(SEq, tte)
                    if wets_free[par] is not None:
                        fs, fv = wets_free[par]
                        w(SEq, fs, fv)
                        w(VEq, fs, fv)
                    emit(SEq, lambda e, par=par, tb3=tb3: e.copy(
                        wets_h[par][:, 0:512], pb[tb3][:]), tse)
                    emit(SEq, lambda e, par=par, tb4=tb4: e.copy(
                        wets_h[par][:, 512:1024], pb[tb4][:]), tse)
                    w(VEq, tse)
                    emit(VEq, lambda e, par=par, tb3=tb3: e.tensor_sub(
                        wets_l[par][:, 0:512], pb[tb3][:],
                        wets_h[par][:, 0:512]), tve)
                    emit(VEq, lambda e, par=par, tb4=tb4: e.tensor_sub(
                        wets_l[par][:, 512:1024], pb[tb4][:],
                        wets_h[par][:, 512:1024]), tve)
                    trot.rel(tb3, tve)
                    trot.rel(tb4, tve)
                    w(GPq, tve)
                    emit(GPq, lambda e, ei=ei, hc=hc, par=par: e.dma_start(
                        out=scr_weth[ei, :, hc, :], in_=wets_h[par][:]), dscw, 16)
                    emit(GPq, lambda e, ei=ei, hc=hc, par=par: e.dma_start(
                        out=scr_wetl[ei, :, hc, :], in_=wets_l[par][:]), dscw, 16)
                    wets_free[par] = (dscw, dscw.n)

        qrelT = poolA[:, 0:4096]
        qrel_tok = poolA[:, 4096:5120]
        qsc = poolA[:, 5120:7168]
        qstack = poolA[:, 7168:9216]
        attn = poolA[:, 9216:10240]
        attnT = poolA[:, 10240:11264]
        krel = poolA[:, 0:4096]
        vcs = poolA[:, 4096:12416]

        kv_first = {}
        wst_ctr = [0]
        xnt_free = (tte, 0)
        acc_free = (tve, 0)
        poolB_free = (tte, tte.n)
        poolA_free = (tte, 0)
        xt_free = [(tte, gate_tr_done), None]
        xnc_free = [None, None]
        xnc_free2 = [None, None]
        wst_free = [None, None]
        pb_exp_free = [None, None]
        xnt_ready = {}
        last_accw = {}

        def xnt_ap(ten, dc, tokgroups, base_ch):
            v = ten.rearrange("p (c q) -> p c q", q=1024)
            return v[:, base_ch:base_ch + tokgroups, dc * 128:dc * 128 + 128]

        def plan_chunk_A(c, i, u_par, seq, sti):
            par = i % 2
            if xt_free[par] is not None:
                fs, fv = xt_free[par]
                w(SYq, fs, fv)
            emit(SYq, lambda e, c=c, par=par: e.dma_start(
                out=xt2[:, par * 1024:par * 1024 + 1024],
                in_=x_in[c * 128:(c + 1) * 128, :]), dx, 16)
            xload = dx.n
            xs = xt2[:, par * 1024:par * 1024 + 1024]
            xn = xnc2[:, 0:1024]
            w(VEq, dx, xload)
            emit(VEq, lambda e, xs=xs: e.bn_stats(stats[:, 0:6], xs[:, 0:512]), tve)
            emit(VEq, lambda e, xs=xs: e.bn_stats(stats[:, 6:12], xs[:, 512:1024]), tve)
            ve_join()
            emit(VEq, lambda e: e.bn_aggr(mvt[:], stats[:]), tve)
            w(SEq, tve)
            emit(SEq, lambda e: e.activation(sq1[:], mvt[:, 1:2], AF.Sqrt,
                                             bias=epsc[:]), tse)
            w(VEq, tse)
            emit(VEq, lambda e: e.reciprocal(rstd[:], sq1[:]), tve)
            ve_join()
            if xnc_free[0] is not None:
                fs, fv = xnc_free[0]
                w(VEq, fs, fv)
            if xnc_free2[0] is not None:
                fs, fv = xnc_free2[0]
                w(VEq, fs, fv)
            emit(VEq, lambda e, xs=xs, xn=xn: e.tensor_scalar(
                xn, xs, mvt[:, 0:1], rstd[:], ALU.subtract, ALU.mult), tve)
            ve_join()
            emit(VEq, lambda e, xs=xs, xn=xn, i=i: e.tensor_add(
                acc_st[:, i * 1024:i * 1024 + 1024], xs, xn), tve)
            xt_free[par] = (tve, tve.n)
            w(TEq, tve)
            b3 = trot.acq()
            for dc in range(4):
                emit(TEq, lambda e, dc=dc, xn=xn, b3=b3: e.transpose(
                    pb[b3][:, dc * 128:dc * 128 + 128],
                    xn[:, dc * 128:dc * 128 + 128], ident[:]), None)
            b4 = trot.acq()
            for dc in range(4):
                emit(TEq, lambda e, dc=dc, xn=xn, b4=b4: e.transpose(
                    pb[b4][:, dc * 128:dc * 128 + 128],
                    xn[:, (dc + 4) * 128:(dc + 4) * 128 + 128], ident[:]), tte)
            w(GPq, tve)
            emit(GPq, lambda e, c=c, xn=xn: e.dma_start(
                out=scr_xtok[c], in_=xn), dxtw, 16)
            xnc_free[0] = (tte, tte.n)
            xnc_free2[0] = (dxtw, dxtw.n)
            w(SEq, tte)
            emit(SEq, lambda e, i=i, b3=b3: e.copy(
                xnth[:, i * 1024:i * 1024 + 512], pb[b3][:]), tse)
            emit(SEq, lambda e, i=i, b4=b4: e.copy(
                xnth[:, i * 1024 + 512:i * 1024 + 1024], pb[b4][:]), tse)
            w(VEq, tse)
            emit(VEq, lambda e, i=i, b3=b3: e.tensor_sub(
                xntl[:, i * 1024:i * 1024 + 512], pb[b3][:],
                xnth[:, i * 1024:i * 1024 + 512]), tve)
            emit(VEq, lambda e, i=i, b4=b4: e.tensor_sub(
                xntl[:, i * 1024 + 512:i * 1024 + 1024], pb[b4][:],
                xnth[:, i * 1024 + 512:i * 1024 + 1024]), tve)
            trot.rel(b3, tve)
            trot.rel(b4, tve)
            xnt_ready[c] = (tve, tve.n)
            w(GPq, tve)
            emit(GPq, lambda e, c=c, i=i: e.dma_start(
                out=scr_xnth[c], in_=xnth[:, i * 1024:i * 1024 + 1024]), dxw, 16)
            emit(GPq, lambda e, c=c, i=i: e.dma_start(
                out=scr_xntl[c], in_=xntl[:, i * 1024:i * 1024 + 1024]), dxw, 16)

        def plan_kv_part(g, isv, c0, seq):
            evac_groups = []
            for ocl in range(4):
                oc = g * 4 + ocl
                ocbase = (2048 if isv else 1024) + oc * 128
                parW = wst_ctr[0] % 2
                wst_ctr[0] += 1
                if wst_free[parW] is not None:
                    fs, fv = wst_free[parW]
                    w(SYq, fs, fv)
                emit(SYq, lambda e, ocbase=ocbase, parW=parW: e.dma_start(
                    out=wsth[:, parW * 1024:parW * 1024 + 1024],
                    in_=scr_wqh.rearrange("(dc p) q -> p dc q", p=128)[
                        :, :, ocbase:ocbase + 128]), dwst, 16)
                emit(SYq, lambda e, ocbase=ocbase, parW=parW: e.dma_start(
                    out=wstl[:, parW * 1024:parW * 1024 + 1024],
                    in_=scr_wql.rearrange("(dc p) q -> p dc q", p=128)[
                        :, :, ocbase:ocbase + 128]), dwst, 16)
                wload = dwst.n
                for tg in range(2):
                    mb = rot.acq()
                    w(TEq, dwst, wload)
                    for dc in range(8):
                        for pi, (wt, xt_) in enumerate(
                                ((wsth, xnth), (wsth, xntl), (wstl, xnth))):
                            emit(TEq, lambda e, dc=dc, parW=parW, tg=tg, mb=mb,
                                 wt=wt, xt_=xt_, pi=pi: e.matmul(
                                     pb[mb][:],
                                     wt[:, parW * 1024 + dc * 128:
                                        parW * 1024 + dc * 128 + 128],
                                     xnt_ap(xt_, dc, 4, tg * 4),
                                     start=(dc == 0 and pi == 0),
                                     stop=(dc == 7 and pi == 2)),
                                 tte if (dc == 7 and pi == 2) else None)
                    bcol = (16 if isv else 8) + oc
                    w(SEq, tte)
                    emit(SEq, lambda e, mb=mb, ocl=ocl, tg=tg, bcol=bcol, isv=isv:
                         e.activation(
                             kvt4[:, ocl * 1024 + tg * 512:ocl * 1024 + tg * 512 + 512],
                             pb[mb][:], AF.Identity if isv else AF.Relu,
                             bias=bqkv_col[:, bcol:bcol + 1]), tse)
                    rot.rel(mb, tse)
                wst_free[parW] = (tte, tte.n)
            kvt_ready = tse.n
            w(TEq, tse, kvt_ready)
            for i in range(CST):
                tb = trot.acq()
                for ocl in range(4):
                    emit(TEq, lambda e, ocl=ocl, i=i, tb=tb: e.transpose(
                        pb[tb][:, ocl * 128:ocl * 128 + 128],
                        kvt4[:, ocl * 1024 + i * 128:ocl * 1024 + i * 128 + 128],
                        ident[:]), tte if ocl == 3 else None)
                posidx = (c0 % CPS) + i
                if not isv:
                    w(VEq, tte)
                    emit(VEq, lambda e, i=i, tb=tb: e.tensor_copy(
                        krel[:, i * 512:i * 512 + 512], pb[tb][:]), tve)
                    trot.rel(tb, tve)
                else:
                    w(SEq, tte)
                    for trig in range(2):
                        tab = costab if trig == 0 else sintab
                        base = trig * 4160 + i * 520
                        emit(SEq, lambda e, tb=tb, base=base, tab=tab, posidx=posidx:
                             e.activation(
                                 vcs[:, base:base + 520].rearrange(
                                     "p (h j) -> p h j", j=65)[:, :, 0:64],
                                 pb[tb][:].rearrange(
                                     "p (h j) -> p h j", j=64),
                                 AF.Identity, scale=tab[:, posidx:posidx + 1]),
                             tse)
                    trot.rel(tb, tse)
                    w(VEq, tse)
                    for trig in range(2):
                        tab = costab if trig == 0 else sintab
                        base = trig * 4160 + i * 520
                        emit(VEq, lambda e, base=base, tab=tab, posidx=posidx:
                             e.tensor_copy(
                                 vcs[:, base:base + 520].rearrange(
                                     "p (h j) -> p h j", j=65)[:, :, 64:65],
                                 tab[:, posidx:posidx + 1].to_broadcast([128, 8, 1])),
                             tve)

        def plan_kv_einsum(g, seq, is_last_supertile):
            w(TEq, tve)
            w(TEq, tse)
            for i in range(CST):
                for hl in range(8):
                    h = g * 8 + hl
                    bk = 5 + (h // 7)
                    co = (h % 7) * 65
                    for trig in range(2):
                        first = kv_first[(bk - 5, trig)]
                        kv_first[(bk - 5, trig)] = False
                        emit(TEq, lambda e, i=i, hl=hl, bk=bk, co=co, trig=trig,
                             first=first: e.matmul(
                                 pb[bk][0:64, co:co + 65] if trig == 0
                                 else pb[bk][64:128, co:co + 65],
                                 krel[:, i * 512 + hl * 64:i * 512 + hl * 64 + 64],
                                 vcs[:, trig * 4160 + i * 520 + hl * 65:
                                     trig * 4160 + i * 520 + hl * 65 + 65],
                                 start=first, stop=False,
                                 tile_position=(0, 64 * trig),
                                 skip_group_check=True),
                             tte if (hl == 7 and trig == 1) else None)

        def plan_gate_masks(c0):
            for i in range(CST):
                c = c0 + i
                mb = rot.acq()
                rs, rv = xnt_ready[c0 + i]
                w(TEq, rs, rv)
                w(TEq, tve, gwt_ready)
                w(TEq, tse, gwt_ready_se)
                for dc in range(8):
                    for pi, (xt_, gt_) in enumerate(
                            ((xnth, gwth), (xnth, gwtl), (xntl, gwth))):
                        emit(TEq, lambda e, i=i, dc=dc, mb=mb, xt_=xt_, gt_=gt_,
                             pi=pi: e.matmul(
                                 pb[mb][:, 0:8],
                                 xt_[:, i * 1024 + dc * 128:i * 1024 + dc * 128 + 128],
                                 gt_[:, dc * 8:dc * 8 + 8],
                                 start=(dc == 0 and pi == 0),
                                 stop=(dc == 7 and pi == 2)),
                             tte if (dc == 7 and pi == 2) else None)
                w(VEq, tte)
                emit(VEq, lambda e, mb=mb: e.reduce_max(mx1[:], pb[mb][:, 0:8],
                                                        axis=AX.X), tve)
                ve_join()
                emit(VEq, lambda e, mb=mb: e.tensor_scalar(
                    m08[:], pb[mb][:, 0:8], mx1[:, 0:1], None, ALU.is_equal), tve)
                rot.rel(mb, tve)
                ve_join()
                emit(VEq, lambda e: e.tensor_tensor_scan(
                    cum8[:], m08[:], zero8[:], 0.0, ALU.add, ALU.add), tve)
                ve_join()
                emit(VEq, lambda e: e.tensor_single_scalar(
                    meq8[:], cum8[:], 1.0, ALU.is_equal), tve)
                ve_join()
                emit(VEq, lambda e, c=c: e.tensor_mul(
                    masks_st[:, c * 8:c * 8 + 8], m08[:], meq8[:]), tve)
                ve_join()
                emit(VEq, lambda e, c=c: e.tensor_mul(
                    tmpcol8[:], masks_st[:, c * 8:c * 8 + 8], iota8f[:]), tve)
                ve_join()
                emit(VEq, lambda e, c=c: e.reduce_sum(
                    chosen_all[:, c:c + 1], tmpcol8[:], axis=AX.X), tve)
                masks_rdy = tve.n
                mbx = rot.acq()
                w(TEq, tve, masks_rdy)
                emit(TEq, lambda e, c=c, mbx=mbx: e.matmul(
                    pb[mbx][0:1, 0:8], onescol[:],
                    masks_st[:, c * 8:c * 8 + 8],
                    start=True, stop=True), tte)
                w(VEq, tte)
                emit(VEq, lambda e, c=c, mbx=mbx: e.tensor_copy(
                    cnt_row[0:1, c * 8:c * 8 + 8], pb[mbx][0:1, 0:8]), tve)
                rot.rel(mbx, tve)


        from concourse.bass import IndirectOffsetOnAxis

        def plan_bookkeeping():
            ve_join()
            cr = cnt_row.rearrange("o (c ee) -> o ee c", ee=8)
            cf = coffx.rearrange("o (c ee) -> o ee c", ee=8)
            for ee in range(8):
                emit(VEq, lambda e, ee=ee: e.tensor_tensor_scan(
                    cf[:, ee, :], cr[:, ee, :], zrow[:], 0.0,
                    ALU.add, ALU.add), tve)
            ve_join()
            emit(VEq, lambda e: e.tensor_copy(tmprow[:], coffx[0:1, 248:256]), tve)
            ve_join()
            emit(VEq, lambda e: e.tensor_sub(coffx[:], coffx[:], cnt_row[:]), tve)
            emit(VEq, lambda e: e.tensor_tensor_scan(
                cumtot[:], tmprow[:], zrow[0:1, 0:8], 0.0, ALU.add, ALU.add), tve)
            ve_join()
            emit(VEq, lambda e: e.tensor_sub(basex[:], cumtot[:], tmprow[:]), tve)
            ve_join()
            emit(VEq, lambda e: e.tensor_add(
                combo_row.rearrange("o (c ee) -> o c ee", ee=8),
                coffx.rearrange("o (c ee) -> o c ee", ee=8),
                basex[:].unsqueeze(1).to_broadcast([1, 32, 8])), tve)
            for j in range(8):
                ve_join()
                emit(VEq, lambda e, j=j: e.tensor_single_scalar(
                    tmprow2[:], cumtot[:], float(j * 512), ALU.is_le), tve)
                ve_join()
                emit(VEq, lambda e, j=j: e.reduce_sum(
                    e1row[0:1, j:j + 1], tmprow2[:], axis=AX.X), tve)
            ve_join()
            emit(VEq, lambda e: e.tensor_single_scalar(
                tbrow[:], basex[:], 512.0, ALU.is_ge), tve)
            for jj in range(2, 8):
                ve_join()
                emit(VEq, lambda e, jj=jj: e.tensor_single_scalar(
                    tmprow2[:], basex[:], float(jj * 512), ALU.is_ge), tve)
                ve_join()
                emit(VEq, lambda e: e.tensor_add(tbrow[:], tbrow[:], tmprow2[:]),
                     tve)
            ve_join()
            emit(VEq, lambda e: e.tensor_scalar(
                tmprow2[:], tbrow[:], 512.0, None, ALU.mult), tve)
            ve_join()
            emit(VEq, lambda e: e.tensor_sub(tmprow2[:], basex[:], tmprow2[:]), tve)
            ve_join()
            emit(VEq, lambda e: e.tensor_single_scalar(
                flagrow[:], tmprow2[:], 0.5, ALU.is_ge), tve)
            rows_ready = tve.n
            for (row, col) in ((e1row, e1col), (tbrow, tbcol), (flagrow, flagcol)):
                mb = rot.acq()
                w(TEq, tve, rows_ready)
                emit(TEq, lambda e, row=row, mb=mb: e.matmul(
                    pb[mb][:, 0:8], ones1[:], row[:],
                    start=True, stop=True), tte)
                w(VEq, tte)
                emit(VEq, lambda e, col=col, mb=mb: e.tensor_copy(
                    col[:], pb[mb][:, 0:8]), tve)
                rot.rel(mb, tve)
            stg_free = [None, None]
            w(SYq, dxtw)
            w(SYq, tte)
            w(SYq, tve)
            for c in range(32):
                mb = rot.acq()
                emit(TEq, lambda e, c=c, mb=mb: e.matmul(
                    pb[mb][:, 0:8], ustrict[:],
                    masks_st[:, c * 8:c * 8 + 8],
                    start=True, stop=True), tte)
                emit(TEq, lambda e, c=c, mb=mb: e.matmul(
                    pb[mb][:, 8:16], ones1[:], combo_row[0:1, c * 8:c * 8 + 8],
                    start=False, stop=True, skip_group_check=True), tte)
                w(VEq, tte)
                ve_join()
                emit(VEq, lambda e, c=c, mb=mb: e.tensor_copy(
                    tmpcol8[:], pb[mb][:, 0:8]), tve)
                ve_join()
                emit(VEq, lambda e, c=c, mb=mb: e.tensor_add(
                    tmpcol8[:], tmpcol8[:], pb[mb][:, 8:16]), tve)
                rot.rel(mb, tve)
                ve_join()
                emit(VEq, lambda e, c=c: e.tensor_mul(
                    tmpcol8[:], tmpcol8[:], masks_st[:, c * 8:c * 8 + 8]), tve)
                ve_join()
                emit(VEq, lambda e, c=c: e.reduce_sum(
                    sq1[:], tmpcol8[:], axis=AX.X), tve)
                ve_join()
                emit(VEq, lambda e, c=c: e.tensor_copy(
                    pos_u32[:, c:c + 1], sq1[:]), tve)
                par = c % 2
                stage = poolA[:, par * 1088:par * 1088 + 1088]
                if stg_free[par] is not None:
                    fs, fv = stg_free[par]
                    w(SYq, fs, fv)
                emit(SYq, lambda e, c=c, stage=stage: e.dma_start(
                    out=stage[:, 0:1024], in_=scr_xtok[c]), dfx, 16)
                ld = dfx.n
                w(VEq, dfx, ld)
                ve_join()
                emit(VEq, lambda e, c=c, stage=stage: e.tensor_copy(
                    stage[:, 1024:1025], chosen_all[:, c:c + 1]), tve)
                w(GPq, tve)
                if c >= 1:
                    w(GPq, dsc, 16 * c - 16)
                emit(GPq, lambda e, c=c, stage=stage: e.indirect_dma_start(
                    out=scr_xsort,
                    out_offset=IndirectOffsetOnAxis(pos_u32[:, c:c + 1], 0),
                    in_=stage[:], in_offset=None), dsc, 16)
                stg_free[par] = (dsc, dsc.n)
            for half in range(2):
                ve_join()
                emit(VEq, lambda e, half=half: e.tensor_scalar(
                    tmpcol8[:], e1col[:], 256.0, float(half),
                    ALU.mult, ALU.add), tve)
                ve_join()
                emit(VEq, lambda e, half=half: e.tensor_tensor(
                    widx[:, half * 8:half * 8 + 8], tmpcol8[:],
                    io2[:].to_broadcast([128, 8]), ALU.add), tve)
            for ee in range(1, 8):
                ve_join()
                emit(VEq, lambda e, ee=ee: e.tensor_scalar(
                    xidxf[:, (ee - 1) * 4:(ee - 1) * 4 + 4],
                    tbcol[:, ee:ee + 1].to_broadcast([128, 4]),
                    512.0, None, ALU.mult), tve)
                ve_join()
                emit(VEq, lambda e, ee=ee: e.tensor_add(
                    xidxf[:, (ee - 1) * 4:(ee - 1) * 4 + 4],
                    xidxf[:, (ee - 1) * 4:(ee - 1) * 4 + 4], iotac4[:]), tve)
            ve_join()
            emit(VEq, lambda e: e.tensor_copy(xidx[:], xidxf[:]), tve)

        def plan_ffn():
            pbB = poolB[:].bitcast(BF16)
            ffx = [poolA[:, 0:1088], poolA[:, 1088:2176]]
            fxnth4 = poolA[:, 2176:4224].bitcast(BF16)
            fxntl4 = poolA[:, 4224:6272].bitcast(BF16)
            ffy4s = [poolA[:, 6272:10368], kvt4[:, 0:4096]]
            mcol4 = poolA[:, 10368:10372]
            wet_rows_h = scr_weth.rearrange(
                "ee p (hh hl) f -> (ee p hh) hl f", hh=2)
            wet_rows_l = scr_wetl.rearrange(
                "ee p (hh hl) f -> (ee p hh) hl f", hh=2)
            pbq_free = [None, None]
            ldctr = [0]
            ffx_free = [None, None]
            ffx_free_te = [None, None]
            fixed_y_done = None

            def load_weights(ent, half):
                buf = ldctr[0] % 2
                ldctr[0] += 1
                q = GPq if ent < 8 else SYq
                if pbq_free[buf] is not None:
                    fs, fv = pbq_free[buf]
                    w(q, fs, fv)
                hi_sl = pbB[:, buf * 8192:buf * 8192 + 4096]
                lo_sl = pbB[:, buf * 8192 + 4096:buf * 8192 + 8192]
                if ent < 8:
                    w(GPq, tve)
                    emit(GPq, lambda e, ent=ent, half=half, hi_sl=hi_sl:
                         e.indirect_dma_start(
                             out=hi_sl,
                             in_=wet_rows_h,
                             out_offset=None,
                             in_offset=IndirectOffsetOnAxis(
                                 widx[:, half * 8 + ent:half * 8 + ent + 1],
                                 0)), dfw, 16)
                    emit(GPq, lambda e, ent=ent, half=half, lo_sl=lo_sl:
                         e.indirect_dma_start(
                             out=lo_sl,
                             in_=wet_rows_l,
                             out_offset=None,
                             in_offset=IndirectOffsetOnAxis(
                                 widx[:, half * 8 + ent:half * 8 + ent + 1],
                                 0)), dfw, 16)
                else:
                    ee = ent - 7
                    emit(SYq, lambda e, ee=ee, half=half, hi_sl=hi_sl:
                         e.dma_start(out=hi_sl,
                                     in_=scr_weth[ee, :, half * 4:half * 4 + 4, :]),
                         dfw, 16)
                    emit(SYq, lambda e, ee=ee, half=half, lo_sl=lo_sl:
                         e.dma_start(out=lo_sl,
                                     in_=scr_wetl[ee, :, half * 4:half * 4 + 4, :]),
                         dfw, 16)
                return buf, dfw.n

            def ffn_bar():
                bar()
                w(GPq, tve)
                w(GPq, tse)
                w(GPq, tte)
                w(SYq, tve)
                w(SYq, tse)
                w(SYq, tte)

            dyw_hist = [0, 0]
            ffy_free = [None, None]
            w(GPq, dyw, dyw_hist[1])
            wl_next = (load_weights(0, 0), load_weights(0, 1))
            for ent in range(15):
                fixed = ent < 8
                ee = ent - 7 if not fixed else None
                ffy4 = ffy4s[ent % 2]
                wl0, wl1 = wl_next
                for ic in range(4):
                    par = ic % 2
                    if ffx_free[par] is not None:
                        fs, fv = ffx_free[par]
                        w(SYq if fixed else GPq, fs, fv)
                    if ffx_free_te[par] is not None:
                        fs, fv = ffx_free_te[par]
                        w(SYq if fixed else GPq, fs, fv)
                    if fixed:
                        w(SYq, dsc)
                        emit(SYq, lambda e, ent=ent, ic=ic, par=par: e.dma_start(
                            out=ffx[par][:],
                            in_=scr_xsort[ent * 512 + ic * 128:
                                          ent * 512 + ic * 128 + 128, :]), dfx, 16)
                    else:
                        w(GPq, dsc)
                        emit(GPq, lambda e, ee=ee, ic=ic, par=par:
                             e.indirect_dma_start(
                                 out=ffx[par][:], out_offset=None,
                                 in_=scr_xsort,
                                 in_offset=IndirectOffsetOnAxis(
                                     xidx[:, (ee - 1) * 4 + ic:
                                          (ee - 1) * 4 + ic + 1], 0)), dfx, 16)
                    xld = dfx.n
                    w(TEq, dfx, xld)
                    b3 = trot.acq()
                    for dc in range(4):
                        emit(TEq, lambda e, dc=dc, par=par, b3=b3: e.transpose(
                            pb[b3][:, dc * 128:dc * 128 + 128],
                            ffx[par][:, dc * 128:dc * 128 + 128], ident[:]), None)
                    b4 = trot.acq()
                    for dc in range(4):
                        emit(TEq, lambda e, dc=dc, par=par, b4=b4: e.transpose(
                            pb[b4][:, dc * 128:dc * 128 + 128],
                            ffx[par][:, (dc + 4) * 128:(dc + 4) * 128 + 128],
                            ident[:]), tte)
                    w(SEq, tte)
                    emit(SEq, lambda e, ic=ic, b3=b3: e.copy(
                        fxnth4[:, ic * 1024:ic * 1024 + 512], pb[b3][:]), tse)
                    emit(SEq, lambda e, ic=ic, b4=b4: e.copy(
                        fxnth4[:, ic * 1024 + 512:ic * 1024 + 1024], pb[b4][:]),
                        tse)
                    w(VEq, tse)
                    emit(VEq, lambda e, ic=ic, b3=b3: e.tensor_sub(
                        fxntl4[:, ic * 1024:ic * 1024 + 512], pb[b3][:],
                        fxnth4[:, ic * 1024:ic * 1024 + 512]), tve)
                    emit(VEq, lambda e, ic=ic, b4=b4: e.tensor_sub(
                        fxntl4[:, ic * 1024 + 512:ic * 1024 + 1024], pb[b4][:],
                        fxnth4[:, ic * 1024 + 512:ic * 1024 + 1024]), tve)
                    trot.rel(b3, tve)
                    trot.rel(b4, tve)
                    ve_join()
                    if fixed:
                        emit(VEq, lambda e, ent=ent, ic=ic, par=par: e.tensor_scalar(
                            mcol4[:, ic:ic + 1], ffx[par][:, 1024:1025],
                            e1col[:, ent:ent + 1], None, ALU.is_equal), tve)
                    else:
                        emit(VEq, lambda e, ee=ee, ic=ic, par=par:
                             e.tensor_single_scalar(
                                 mcol4[:, ic:ic + 1], ffx[par][:, 1024:1025],
                                 float(ee), ALU.is_equal), tve)
                        ve_join()
                        emit(VEq, lambda e, ee=ee, ic=ic: e.tensor_mul(
                            mcol4[:, ic:ic + 1], mcol4[:, ic:ic + 1],
                            flagcol[:, ee:ee + 1]), tve)
                    ffx_free[par] = (tve, tve.n)
                    ffx_free_te[par] = (tte, tte.n)
                fx_ready = tve.n
                for half, (buf, wld) in enumerate((wl0, wl1)):
                    first_w = True
                    for ic in range(4):
                        for fg in range(2):
                            mb = rot.acq()
                            if first_w:
                                w(TEq, dfw, wld)
                                w(TEq, tve, fx_ready)
                                first_w = False
                            for hcl in range(4):
                                for pi, (xt_, wo_) in enumerate(
                                        ((fxnth4, 0), (fxnth4, 4096),
                                         (fxntl4, 0))):
                                    emit(TEq, lambda e, ic=ic, fg=fg, hcl=hcl,
                                         buf=buf, xt_=xt_, wo_=wo_, pi=pi,
                                         half=half, mb=mb: e.matmul(
                                             pb[mb][:],
                                             xt_[:, ic * 1024 + (half * 4 + hcl) * 128:
                                                 ic * 1024 + (half * 4 + hcl) * 128 + 128],
                                             pbB[:, buf * 8192 + wo_ + hcl * 1024 + fg * 512:
                                                 buf * 8192 + wo_ + hcl * 1024 + fg * 512 + 512],
                                             start=(hcl == 0 and pi == 0),
                                             stop=(hcl == 3 and pi == 2)),
                                         tte if (hcl == 3 and pi == 2) else None)
                            if half == 0:
                                w(SEq, tte)
                                if ffy_free[ent % 2] is not None:
                                    fs, fv = ffy_free[ent % 2]
                                    w(SEq, fs, fv)
                                emit(SEq, lambda e, ic=ic, fg=fg, mb=mb,
                                     ffy4=ffy4: e.copy(
                                         ffy4[:, ic * 1024 + fg * 512:
                                              ic * 1024 + fg * 512 + 512],
                                         pb[mb][:]), tse)
                                rot.rel(mb, tse)
                            else:
                                w(VEq, tte)
                                ve_join()
                                w(VEq, tse)
                                emit(VEq, lambda e, ic=ic, fg=fg, mb=mb,
                                     ffy4=ffy4: e.tensor_add(
                                         ffy4[:, ic * 1024 + fg * 512:
                                              ic * 1024 + fg * 512 + 512],
                                         ffy4[:, ic * 1024 + fg * 512:
                                              ic * 1024 + fg * 512 + 512],
                                         pb[mb][:]), tve)
                                rot.rel(mb, tve)
                pbq_free[wl0[0]] = (tte, tte.n)
                pbq_free[wl1[0]] = (tte, tte.n)
                if ent < 14:
                    w(GPq, dyw, dyw_hist[1])
                    nfixed = (ent + 1) < 8
                    wl_next = (load_weights(ent + 1, 0), load_weights(ent + 1, 1))
                for ic in range(4):
                    ve_join()
                    emit(VEq, lambda e, ic=ic, ffy4=ffy4: e.tensor_scalar(
                        ffy4[:, ic * 1024:ic * 1024 + 1024],
                        ffy4[:, ic * 1024:ic * 1024 + 1024],
                        mcol4[:, ic:ic + 1], None, ALU.mult), tve)
                    w(GPq, tve)
                    if fixed:
                        emit(GPq, lambda e, ent=ent, ic=ic, ffy4=ffy4: e.dma_start(
                            out=scr_y[ent * 512 + ic * 128:
                                      ent * 512 + ic * 128 + 128, :],
                            in_=ffy4[:, ic * 1024:ic * 1024 + 1024]), dyw, 16)
                    else:
                        w(GPq, dyw)
                        emit(GPq, lambda e, ee=ee, ic=ic, ffy4=ffy4:
                             e.indirect_dma_start(
                                 out=scr_y,
                                 out_offset=IndirectOffsetOnAxis(
                                     xidx[:, (ee - 1) * 4 + ic:
                                          (ee - 1) * 4 + ic + 1], 0),
                                 in_=ffy4[:, ic * 1024:ic * 1024 + 1024],
                                 in_offset=None, compute_op=ALU.add), dyw, 16)
                dyw_hist[0] = dyw_hist[1]
                dyw_hist[1] = dyw.n
                ffy_free[ent % 2] = (dyw, dyw.n)

        out_dmas = 0
        kvstack_ready_l = {}
        for seq in range(SEQS):
            for _b in range(3):
                kv_first[(_b, 0)] = True
                kv_first[(_b, 1)] = True
            for sti in range(NST):
                c0 = seq * CPS + sti * CST
                if (seq, sti) != (0, 0):
                    w(VEq, daw)
                    w(SEq, dxw)
                for i in range(CST):
                    plan_chunk_A(c0 + i, i, None, seq, sti)
                for g in range(2):
                    plan_kv_part(g, 0, c0, seq)
                    plan_kv_part(g, 1, c0, seq)
                    plan_kv_einsum(g, seq, sti == NST - 1)
                plan_gate_masks(c0)
                w(GPq, tve)
                emit(GPq, lambda e, c0=c0: e.dma_start(
                    out=scr_acc[c0:c0 + CST].rearrange("c p f -> p c f"),
                    in_=acc_st[:]), daw, 16)
                if (seq, sti) == (0, 0):
                    plan_expert_prep()
            w(VEq, tte)
            emit(VEq, lambda e, seq=seq: e.tensor_copy(
                kvstack[:, seq * 1040:seq * 1040 + 455], pb[5][:, 0:455]), tve)
            emit(VEq, lambda e, seq=seq: e.tensor_copy(
                kvstack[:, seq * 1040 + 455:seq * 1040 + 910],
                pb[6][:, 0:455]), tve)
            emit(VEq, lambda e, seq=seq: e.tensor_copy(
                kvstack[:, seq * 1040 + 910:seq * 1040 + 1040],
                pb[7][:, 0:130]), tve)
            kvstack_ready_l[seq] = tve.n

        plan_bookkeeping()
        plan_ffn()
        ffn_done = dyw.n
        w(SYq, tte)
        emit(SYq, lambda e: e.dma_start(
            out=poolB[:],
            in_=wout.rearrange("(c p) f -> p c f", p=128)), dwe, 16)
        wout_ready = dwe.n

        for seq in range(SEQS):
            kvstack_ready = kvstack_ready_l[seq]
            if True:
                for tg in range(4):
                    cg0 = seq * CPS + tg * 4
                    w(GPq, tte)
                    w(GPq, dxw)
                    emit(GPq, lambda e, cg0=cg0: e.dma_start(
                        out=xnth[:, 0:4096],
                        in_=scr_xnth[cg0:cg0 + 4].rearrange("c p f -> p c f")),
                        dxr, 16)
                    emit(GPq, lambda e, cg0=cg0: e.dma_start(
                        out=xntl[:, 0:4096],
                        in_=scr_xntl[cg0:cg0 + 4].rearrange("c p f -> p c f")),
                        dxr, 16)
                    w(GPq, daw)
                    w(GPq, dout)
                    emit(GPq, lambda e, cg0=cg0: e.dma_start(
                        out=acc_st[:, 0:4096],
                        in_=scr_acc[cg0:cg0 + 4].rearrange("c p f -> p c f")),
                        dar, 16)
                    xnt2_ready = dxr.n
                    acc2_ready = dar.n
                    for qc in range(8):
                        parW = wst_ctr[0] % 2
                        wst_ctr[0] += 1
                        if wst_free[parW] is not None:
                            fs, fv = wst_free[parW]
                            w(SYq, fs, fv)
                        emit(SYq, lambda e, qc=qc, parW=parW: e.dma_start(
                            out=wsth[:, parW * 1024:parW * 1024 + 1024],
                            in_=scr_wqh.rearrange("(dc p) q -> p dc q", p=128)[
                                :, :, qc * 128:qc * 128 + 128]), dwst, 16)
                        emit(SYq, lambda e, qc=qc, parW=parW: e.dma_start(
                            out=wstl[:, parW * 1024:parW * 1024 + 1024],
                            in_=scr_wql.rearrange("(dc p) q -> p dc q", p=128)[
                                :, :, qc * 128:qc * 128 + 128]), dwst, 16)
                        wload = dwst.n
                        mb = rot.acq()
                        w(TEq, dwst, wload)
                        w(TEq, dxr, xnt2_ready)
                        for dc in range(8):
                            for pi, (wt, xt_) in enumerate(
                                    ((wsth, xnth), (wsth, xntl), (wstl, xnth))):
                                emit(TEq, lambda e, dc=dc, parW=parW, mb=mb,
                                     wt=wt, xt_=xt_, pi=pi: e.matmul(
                                         pb[mb][:],
                                         wt[:, parW * 1024 + dc * 128:
                                            parW * 1024 + dc * 128 + 128],
                                         xnt_ap(xt_, dc, 4, 0),
                                         start=(dc == 0 and pi == 0),
                                         stop=(dc == 7 and pi == 2)),
                                     tte if (dc == 7 and pi == 2) else None)
                        w(SEq, tte)
                        emit(SEq, lambda e, qc=qc, mb=mb: e.activation(
                            qrelT[:, qc * 512:qc * 512 + 512], pb[mb][:], AF.Relu,
                            bias=bqkv_col[:, qc:qc + 1]), tse)
                        rot.rel(mb, tse)
                        wst_free[parW] = (tte, tte.n)
                    qrelT_ready = tse.n
                    for ic in range(4):
                        posidx = tg * 4 + ic
                        w(TEq, tse, qrelT_ready)
                        tb3 = trot.acq()
                        for qc in range(4):
                            emit(TEq, lambda e, qc=qc, ic=ic, tb3=tb3: e.transpose(
                                pb[tb3][:, qc * 128:qc * 128 + 128],
                                qrelT[:, qc * 512 + ic * 128:qc * 512 + ic * 128 + 128],
                                ident[:]), None)
                        tb4 = trot.acq()
                        for qc in range(4):
                            emit(TEq, lambda e, qc=qc, ic=ic, tb4=tb4: e.transpose(
                                pb[tb4][:, qc * 128:qc * 128 + 128],
                                qrelT[:, (qc + 4) * 512 + ic * 128:
                                      (qc + 4) * 512 + ic * 128 + 128],
                                ident[:]), tte)
                        w(VEq, tte)
                        emit(VEq, lambda e, tb3=tb3: e.tensor_copy(
                            qrel_tok[:, 0:512], pb[tb3][:]), tve)
                        emit(VEq, lambda e, tb4=tb4: e.tensor_copy(
                            qrel_tok[:, 512:1024], pb[tb4][:]), tve)
                        trot.rel(tb3, tve)
                        trot.rel(tb4, tve)
                        ve_join()
                        for trig in range(2):
                            tab = costab if trig == 0 else sintab
                            emit(VEq, lambda e, trig=trig, tab=tab, posidx=posidx:
                                 e.tensor_scalar(
                                     qsc.rearrange(
                                         "p (h j) -> p h j", j=128)[
                                         :, :, trig * 64:trig * 64 + 64],
                                     qrel_tok[:, 0:1024].rearrange(
                                         "p (h j) -> p h j", j=64),
                                     tab[:, posidx:posidx + 1], None, ALU.mult),
                                 tve)
                        qsc_ready = tve.n
                        w(TEq, tve, qsc_ready)
                        for grp in range(4):
                            tb = trot.acq()
                            for k2 in range(4):
                                hh = grp * 4 + k2
                                emit(TEq, lambda e, hh=hh, k2=k2, tb=tb: e.transpose(
                                    pb[tb][:, k2 * 128:k2 * 128 + 128],
                                    qsc[:, hh * 128:hh * 128 + 128],
                                    ident[:]), tte if k2 == 3 else None)
                            w(VEq, tte)
                            emit(VEq, lambda e, grp=grp, tb=tb: e.tensor_copy(
                                qstack[:, grp * 512:grp * 512 + 512], pb[tb][:]),
                                tve)
                            trot.rel(tb, tve)
                        qstack_ready = tve.n
                        w(TEq, tve, qstack_ready)
                        w(TEq, tve, kvstack_ready)
                        ctx_banks = []
                        for h in range(16):
                            bk = h // 7
                            if bk >= len(ctx_banks):
                                ctx_banks.append(rot.acq())
                            mb = ctx_banks[bk]
                            co = (h % 7) * 65
                            emit(TEq, lambda e, h=h, mb=mb, co=co, seq=seq:
                                 e.matmul(
                                     pb[mb][:, co:co + 65],
                                     qstack[:, h * 128:h * 128 + 128],
                                     kvstack[:, seq * 1040 + h * 65:
                                             seq * 1040 + h * 65 + 65],
                                     start=True, stop=True),
                                 tte if h == 15 else None)
                        w(VEq, tte)
                        for bi, nh in ((0, 7), (1, 7), (2, 2)):
                            mb = ctx_banks[bi]
                            emit(VEq, lambda e, mb=mb, bi=bi, nh=nh:
                                 e.tensor_single_scalar(
                                     rec16[:, bi * 7:bi * 7 + nh].rearrange(
                                         "p (h j) -> p h j", j=1),
                                     pb[mb][:, 0:nh * 65].rearrange(
                                         "p (h j) -> p h j", j=65)[:, :, 64:65],
                                     EPS_ATT, ALU.add), tve)
                        ve_join()
                        emit(VEq, lambda e: e.reciprocal(rec16b[:], rec16[:]), tve)
                        rec_ready = tve.n
                        w(SEq, tve, rec_ready)
                        for h in range(16):
                            mb = ctx_banks[h // 7]
                            co = (h % 7) * 65
                            emit(SEq, lambda e, h=h, mb=mb, co=co: e.activation(
                                attn[:, h * 64:h * 64 + 64], pb[mb][:, co:co + 64],
                                AF.Identity, scale=rec16b[:, h:h + 1]), tse)
                        for mb in ctx_banks:
                            rot.rel(mb, tse)
                        attn_ready = tse.n
                        w(TEq, tse, attn_ready)
                        tb3 = trot.acq()
                        for ec in range(4):
                            emit(TEq, lambda e, ec=ec, tb3=tb3: e.transpose(
                                pb[tb3][:, ec * 128:ec * 128 + 128],
                                attn[:, ec * 128:ec * 128 + 128], ident[:]), None)
                        tb4 = trot.acq()
                        for ec in range(4):
                            emit(TEq, lambda e, ec=ec, tb4=tb4: e.transpose(
                                pb[tb4][:, ec * 128:ec * 128 + 128],
                                attn[:, (ec + 4) * 128:(ec + 4) * 128 + 128],
                                ident[:]), tte)
                        w(VEq, tte)
                        emit(VEq, lambda e, tb3=tb3: e.tensor_copy(
                            attnT[:, 0:512], pb[tb3][:]), tve)
                        emit(VEq, lambda e, tb4=tb4: e.tensor_copy(
                            attnT[:, 512:1024], pb[tb4][:]), tve)
                        trot.rel(tb3, tve)
                        trot.rel(tb4, tve)
                        attnT_ready = tve.n
                        for fg in range(2):
                            mb = rot.acq()
                            w(TEq, tve, attnT_ready)
                            if seq == 0 and tg == 0 and ic == 0 and fg == 0:
                                w(TEq, dwe, wout_ready)
                            for ec in range(8):
                                emit(TEq, lambda e, ec=ec, fg=fg, mb=mb: e.matmul(
                                    pb[mb][:],
                                    attnT[:, ec * 128:ec * 128 + 128],
                                    poolB[:, ec * 1024 + fg * 512:
                                          ec * 1024 + fg * 512 + 512],
                                    start=(ec == 0), stop=False), None)
                            emit(TEq, lambda e, fg=fg, mb=mb: e.matmul(
                                pb[mb][:], ones1[:],
                                bout_row[0:1, fg * 512:fg * 512 + 512],
                                start=False, stop=True), tte)
                            w(VEq, tte)
                            w(VEq, dar, acc2_ready)
                            ve_join()
                            emit(VEq, lambda e, ic=ic, fg=fg, mb=mb: e.tensor_add(
                                acc_st[:, ic * 1024 + fg * 512:
                                       ic * 1024 + fg * 512 + 512],
                                acc_st[:, ic * 1024 + fg * 512:
                                       ic * 1024 + fg * 512 + 512],
                                pb[mb][:]), tve)
                            rot.rel(mb, tve)
                        ygat = poolA[:, 11264:12288]
                        w(GPq, tve)
                        w(GPq, dyw, ffn_done)
                        emit(GPq, lambda e, cg0=cg0, ic=ic, ygat=ygat:
                             e.indirect_dma_start(
                                 out=ygat, out_offset=None,
                                 in_=scr_y,
                                 in_offset=IndirectOffsetOnAxis(
                                     pos_u32[:, cg0 + ic:cg0 + ic + 1], 0)),
                             dyr, 16)
                        ygld = dyr.n
                        w(VEq, dyr, ygld)
                        ve_join()
                        emit(VEq, lambda e, ic=ic, ygat=ygat: e.tensor_add(
                            acc_st[:, ic * 1024:ic * 1024 + 1024],
                            acc_st[:, ic * 1024:ic * 1024 + 1024], ygat), tve)
                    w(GPq, tve)
                    w(GPq, dyr)
                    emit(GPq, lambda e, cg0=cg0: e.dma_start(
                        out=out_t[cg0 * 128:cg0 * 128 + 512, :].rearrange(
                            "(c p) f -> p c f", p=128),
                        in_=acc_st[:, 0:4096]), dout, 16)
                    out_dmas += 1

        if STAGE < 4:
            w(GPq, daw)
            for c in range(32):
                emit(GPq, lambda e, c=c: e.dma_start(
                    out=out_t[c * 128:(c + 1) * 128, :], in_=scr_acc[c]), dout, 16)
                out_dmas += 1

        if DBG:
            w(GPq, tve)
            emit(GPq, lambda e: e.dma_start(out=o_pos[:], in_=pos_u32[:]), dout, 16)
            emit(GPq, lambda e: e.dma_start(out=o_chosen[:], in_=chosen_all[:]),
                 dout, 16)
            emit(GPq, lambda e: e.dma_start(out=o_rows[0:1, 0:256], in_=cnt_row[:]),
                 dout, 16)
            emit(GPq, lambda e: e.dma_start(out=o_rows[0:1, 256:512], in_=coffx[:]),
                 dout, 16)
            emit(GPq, lambda e: e.dma_start(
                out=o_rows[0:1, 512:520], in_=cumtot[:]), dout, 16)
            emit(GPq, lambda e: e.dma_start(
                out=o_rows[0:1, 520:528], in_=basex[:]), dout, 16)
            emit(GPq, lambda e: e.dma_start(
                out=o_rows[0:1, 528:536], in_=e1row[:]), dout, 16)
            emit(GPq, lambda e: e.dma_start(
                out=o_rows[0:1, 536:544], in_=tbrow[:]), dout, 16)
            emit(GPq, lambda e: e.dma_start(
                out=o_rows[0:1, 544:552], in_=flagrow[:]), dout, 16)
            emit(GPq, lambda e: e.dma_start(out=o_kvs[:], in_=kvstack[:]), dout, 16)
            out_dmas += 1
            w(GPq, dyw)
            for cc in range(32):
                emit(GPq, lambda e, cc=cc: e.dma_start(
                    out=o_y[cc * 128:(cc + 1) * 128, :], in_=scr_y[cc * 128:(cc + 1) * 128, :]), dout, 16)
                emit(GPq, lambda e, cc=cc: e.dma_start(
                    out=o_xs[cc * 128:(cc + 1) * 128, :], in_=scr_xsort[cc * 128:(cc + 1) * 128, :]), dout, 16)
            out_dmas += 9 + 64
        w(GPq, dout, out_dmas * 16)

        block = es.enter_context(nc.Block())

        @block.sync
        def _(e):
            for fn in SYq:
                fn(e)

        @block.tensor
        def _(e):
            for fn in TEq:
                fn(e)

        @block.vector
        def _(e):
            for fn in VEq:
                fn(e)

        @block.scalar
        def _(e):
            for fn in SEq:
                fn(e)

        @block.gpsimd
        def _(e):
            for fn in GPq:
                fn(e)

    return nc


_NC_CACHE = None


def kernel(L0_attn_x, L0_attn_W_qkv, L0_attn_b_qkv, L0_attn_W_out, L0_attn_b_out,
           L0_ffn_gate_w, L0_ffn_expert_w):
    global _NC_CACHE
    x = np.ascontiguousarray(np.asarray(L0_attn_x, dtype=np.float32))
    wq = np.ascontiguousarray(np.asarray(L0_attn_W_qkv, dtype=np.float32))
    bq = np.ascontiguousarray(np.asarray(L0_attn_b_qkv, dtype=np.float32)).reshape(1, 3 * E)
    wo = np.ascontiguousarray(np.asarray(L0_attn_W_out, dtype=np.float32))
    bo = np.ascontiguousarray(np.asarray(L0_attn_b_out, dtype=np.float32)).reshape(1, E)
    gw = np.ascontiguousarray(np.asarray(L0_ffn_gate_w, dtype=np.float32))
    ew = np.ascontiguousarray(np.asarray(L0_ffn_expert_w, dtype=np.float32))
    if _NC_CACHE is None:
        _NC_CACHE = build_nc()
    nc = _NC_CACHE
    in_maps = []
    for c in range(NCORE):
        xc = np.ascontiguousarray(x[2 * c:2 * c + 2].reshape(T, E))
        in_maps.append({"x": xc, "wqkv": wq, "bqkv": bq, "wout": wo, "bout": bo,
                        "gate": gw, "expw": ew})
    res = run_bass_kernel_spmd(nc, in_maps, core_ids=list(range(NCORE)))
    out = np.stack([np.asarray(res.results[c]["out"]) for c in range(NCORE)])
    return np.ascontiguousarray(out.reshape(B, L, E))
